# version 08 (final)
# baseline (speedup 1.0000x reference)
import numpy as np
import ml_dtypes
from contextlib import ExitStack

import concourse.bass as bass
import concourse.mybir as mybir
from concourse.bass_utils import run_bass_kernel_spmd

F32 = mybir.dt.float32
BF16 = mybir.dt.bfloat16
AF = mybir.ActivationFunctionType
ALU = mybir.AluOpType
AX = mybir.AxisListType

L = 2048
H = 2048
HB = 16
TB = 16
TC = 256
NCH = L // TC
DTR = 64
NST = 16
FEXP = 768
NEXP = 32
SF = 5632
EPS = 1e-5
NT5 = L // 512
DEDUP = True


class Sem:
    def __init__(self, ctx, nc, name):
        self.h = ctx.enter_context(nc.semaphore(name))
        self.n = 0

    def bump(self, inst, k=1):
        inst.then_inc(self.h, k)
        self.n += k
        return self.n


def build(stage=7, taps=()):
    taps = set(taps)
    nc = bass.Bass("TRN2", target_bir_lowering=False)

    def I(name, shape):
        return nc.dram_tensor(name, shape, F32, kind="ExternalInput").ap()

    x_in = I("x", [L, H])
    norm_w = I("norm_w", [H])
    ipw = I("in_proj_w", [2 * H, H])
    convw = I("conv_w", [4, H, H])
    convb = I("conv_b", [H])
    xpw = I("x_proj_w", [96, H])
    dtw = I("dt_proj_w", [H, DTR])
    dtb = I("dt_proj_b", [H])
    alog = I("A_log", [H, NST])
    dvec = I("D", [H])
    opw = I("out_proj_w", [H, H])
    gw = I("gate_w", [H, NEXP])
    wg_w = I("wg", [NEXP, H, FEXP])
    wu_w = I("wu", [NEXP, H, FEXP])
    wd_w = I("wd", [NEXP, FEXP, H])
    sg_w = I("sg", [H, SF])
    su_w = I("su", [H, SF])
    sd_w = I("sd", [SF, H])
    seg_w = I("seg", [H, 1])

    y_out = nc.dram_tensor("y", [L, H], F32, kind="ExternalOutput").ap()
    zk = "ExternalOutput" if "z" in taps else "Internal"
    mk = "ExternalOutput" if "mo" in taps else "Internal"
    z_sp = nc.dram_tensor("z_sp", [H, L], BF16, kind=zk).ap()
    mo_sp = nc.dram_tensor("mo_sp", [L, H], BF16, kind=mk).ap()
    tap_t = {}
    for t in taps - {"z", "mo"}:
        tap_t[t] = nc.dram_tensor("tap_" + t, [128, HB * 2048], BF16,
                                  kind="ExternalOutput").ap()

    ident_d = nc.inline_tensor(np.eye(128, dtype=ml_dtypes.bfloat16), "identc").ap()
    CONSTV = [EPS, 1.0, -1.0, -0.5, 1.0 / H, 0.0, 2.0, 0.5]
    cons_d = nc.inline_tensor(
        np.tile(np.array(CONSTV, np.float32), (128, 1)), "consv").ap()
    sel_np = np.zeros((16, 16 * 128), dtype=ml_dtypes.bfloat16)
    for n in range(16):
        sel_np[n, n * 128:(n + 1) * 128] = 1
    sel_d = nc.inline_tensor(sel_np, "selc").ap()

    with ExitStack() as ctx:
        e = ctx.enter_context
        AR1 = nc.alloc_sbuf_tensor("AR1", [128, HB * 2048], BF16)
        AR2 = nc.alloc_sbuf_tensor("AR2", [128, HB * 2048], BF16)
        ident = nc.alloc_sbuf_tensor("ident", [128, 128], BF16)
        sel = nc.alloc_sbuf_tensor("sel", [16, 2048], BF16)
        mean_c = nc.alloc_sbuf_tensor("mean_c", [128, 16], F32)
        rstd_c = nc.alloc_sbuf_tensor("rstd_c", [128, 16], F32)
        nmr_c = nc.alloc_sbuf_tensor("nmr_c", [128, 16], F32)
        w_c = nc.alloc_sbuf_tensor("w_c", [128, 16], F32)
        cb_c = nc.alloc_sbuf_tensor("cb_c", [128, 16], F32)
        dtb_c = nc.alloc_sbuf_tensor("dtb_c", [128, 16], F32)
        d_c = nc.alloc_sbuf_tensor("d_c", [128, 16], F32)
        s_c = nc.alloc_sbuf_tensor("s_c", [128, 16], F32)
        a_c = nc.alloc_sbuf_tensor("a_c", [128, 256], F32)
        comb = nc.alloc_sbuf_tensor("comb", [128, 512], F32)
        gateT = nc.alloc_sbuf_tensor("gateT", [128, 512], BF16)
        segT = nc.alloc_sbuf_tensor("segT", [128, 16], BF16)
        xpwT = nc.alloc_sbuf_tensor("xpwT", [128, 16 * 96], BF16)
        dtwT = nc.alloc_sbuf_tensor("dtwT", [64, 2048], BF16)
        bmcm = nc.alloc_sbuf_tensor("bmcm", [16, 512], BF16)
        din = nc.alloc_sbuf_tensor("din", [64, 256], BF16)
        stat = nc.alloc_sbuf_tensor("stat", [128, 256], F32)
        stbf = nc.alloc_sbuf_tensor("stbf", [128, 2048], BF16)
        smt = nc.alloc_sbuf_tensor("smt", [128, 64], F32)
        mx8 = nc.alloc_sbuf_tensor("mx8", [128, 8], F32)
        cons = nc.alloc_sbuf_tensor("cons", [128, 8], F32)
        for ci_, cv_ in enumerate(CONSTV):
            nc.const_aps.register(F32, cv_, cons[:, ci_:ci_ + 1])

        FLEX = 57344
        fl = nc.bump_sbuf(FLEX)
        assert fl is not None, "flex arena alloc failed"
        fb = fl[0]

        def flex(name, shape, dtype, off):
            return nc.alloc_sbuf_tensor_at(name, shape, dtype, offset=fb + off)

        p0_x = [flex("p0_x%d" % i, [128, 2048], F32, i * 8192) for i in range(2)]
        p0_xn = flex("p0_xn", [128, 2048], F32, 16384)
        p0_sq = flex("p0_sq", [128, 2048], F32, 24576)
        p0_rtm = flex("p0_rtm", [128, 2048], BF16, 32768)

        p1_row = [flex("p1_row%d" % i, [128, 2048], BF16, i * 4096) for i in range(2)]
        p1_ipt = [flex("p1_ipt%d" % i, [128, 2048], BF16, 8192 + i * 4096)
                  for i in range(2)]

        p2_cw = [flex("p2_cw%d" % i, [128, 8192], BF16, i * 16384) for i in range(2)]

        p3_oprow = [flex("p3_oprow%d" % i, [128, 2048], BF16, i * 4096)
                    for i in range(2)]
        p3_xpr = flex("p3_xpr", [96, 2048], BF16, 8192)
        p3_dtr = flex("p3_dtr", [128, 1024], BF16, 12288)

        p3_bc = flex("p3_bc", [128, 16 * 512], BF16, 0)
        p3_z = flex("p3_z", [128, 4096], BF16, 16384)
        p3_yg = flex("p3_yg", [128, 4096], BF16, 24576)
        p3_sz = flex("p3_sz", [128, 4096], BF16, 32768)
        p3_dl = [flex("p3_dl%d" % i, [128, 256], F32, 40960 + i * 1024)
                 for i in range(4)]
        p3_a = [flex("p3_a%d" % i, [128, 256], F32, 45056 + i * 1024)
                for i in range(3)]
        p3_du = flex("p3_du", [128, 256], F32, 48128)
        p3_xs = [flex("p3_xs%d" % i, [128, 256], F32, 49152 + i * 1024)
                 for i in range(2)]
        p3_tl = [flex("p3_tl%d" % i, [128, 256], F32, 51200 + i * 1024)
                 for i in range(2)]
        p3_hs4 = flex("p3_hs4", [128, 1024], BF16, 53248)
        p3_ya = flex("p3_ya", [128, 256], BF16, 55296)
        p3_y2 = flex("p3_y2", [128, 256], BF16, 55808)
        p3_y = flex("p3_y", [128, 256], F32, 56320)

        p4_x = [flex("p4_x%d" % i, [128, 2048], F32, i * 8192) for i in range(2)]
        p4_mo = [flex("p4_mo%d" % i, [128, 2048], BF16, 16384 + i * 4096)
                 for i in range(2)]
        p4_xn = flex("p4_xn", [128, 2048], F32, 24576)
        p4_t1 = flex("p4_t1", [128, 2048], F32, 32768)

        p5_he = flex("p5_he", [128, 6 * 2048], BF16, 0)
        p5_wg = [flex("p5_wg%d" % i, [128, 2048], BF16, 24576 + i * 4096)
                 for i in range(2)]
        p5_wu = [flex("p5_wu%d" % i, [128, 2048], BF16, 32768 + i * 4096)
                 for i in range(2)]
        p5_wd = [flex("p5_wd%d" % i, [128, 3072], BF16, 40960 + i * 6144)
                 for i in range(2)]
        p5_sg = [flex("p5_sg%d" % i, [128, 512], F32, 53248 + i * 2048)
                 for i in range(2)]

        p6_hs = flex("p6_hs", [128, 4 * 2048], BF16, 0)
        p6_wg = [flex("p6_wg%d" % i, [128, 2048], BF16, 16384 + i * 4096)
                 for i in range(2)]
        p6_wu = [flex("p6_wu%d" % i, [128, 2048], BF16, 24576 + i * 4096)
                 for i in range(2)]
        p6_wd = [flex("p6_wd%d" % i, [128, 2048], BF16, 32768 + i * 4096)
                 for i in range(2)]

        p7_o = [flex("p7_o%d" % i, [128, 2048], F32, i * 8192) for i in range(2)]

        qc = Sem(ctx, nc, "qc")
        qa = [Sem(ctx, nc, "qa%d" % i) for i in range(2)]
        qb = [Sem(ctx, nc, "qb%d" % i) for i in range(2)]
        qs = [Sem(ctx, nc, "qs%d" % i) for i in range(4)]
        qw_ = [Sem(ctx, nc, "qw%d" % i) for i in range(2)]
        qg = [Sem(ctx, nc, "qg%d" % i) for i in range(2)]
        qu = [Sem(ctx, nc, "qu%d" % i) for i in range(2)]
        qd = [Sem(ctx, nc, "qd%d" % i) for i in range(2)]
        mm = Sem(ctx, nc, "mm")
        ea = Sem(ctx, nc, "ea")
        ev = Sem(ctx, nc, "ev")
        sea = Sem(ctx, nc, "sea")
        sev = Sem(ctx, nc, "sev")
        SEMS = [qc] + qa + qb + qs + qw_ + qg + qu + qd + [mm, ea, ev]
        OWN = {"te": mm, "sc": ea, "v": ev}
        last_wait = {}
        snap = {}

        def take_snap():
            for s in SEMS:
                snap[s.h.name] = s.n

        def W(eng, tag, s, v=None):
            v = s.n if v is None else v
            if v is None or v <= 0:
                return
            k = (tag, s.h.name)
            if DEDUP and last_wait.get(k, -1) >= v:
                return
            last_wait[k] = v
            eng.wait_ge(s.h, v)

        def gate(eng, tag):
            for s in SEMS:
                if OWN.get(tag) is s:
                    continue
                W(eng, tag, s, snap[s.h.name])

        block = e(nc.Block())

        @block.sync
        def _(sy):
            sy.dma_start(out=ident[:], in_=ident_d).then_inc(qc.h, 16)
            sy.dma_start(out=cons[:], in_=cons_d).then_inc(qc.h, 16)
            sy.dma_start(out=sel[:], in_=sel_d).then_inc(qc.h, 16)
            for vec, dst in ((norm_w, w_c), (convb, cb_c), (dtb, dtb_c),
                             (dvec, d_c)):
                for hb in range(HB):
                    sy.dma_start(out=dst[:, hb:hb + 1],
                                 in_=vec[hb * 128:(hb + 1) * 128]
                                 ).then_inc(qc.h, 16)
            sy.dma_start(out=a_c[:].rearrange("p (hb n) -> p hb n", n=16),
                         in_=alog.rearrange("(hb p) n -> p hb n", p=128)
                         ).then_inc(qc.h, 16)
            qc.n += 16 * 68

        @block.gpsimd
        def _(gp):
            gp.dma_start(out=gateT[:].rearrange("p (hb e) -> p hb e", e=32),
                         in_=gw.rearrange("(hb p) e -> p hb e", p=128)
                         ).then_inc(qg[0].h, 16)
            qg[0].n += 16
            gp.dma_start(out=segT[:].rearrange("p (hb o) -> p hb o", o=1),
                         in_=seg_w.rearrange("(hb p) o -> p hb o", p=128),
                         allow_slow_non_contiguous=True).then_inc(qg[1].h, 16)
            qg[1].n += 16
        const_qx = qc.n

        @block.scalar
        def _(sc):
            W(sc, "sc", qc, const_qx)
            ea.bump(sc.activation(out=p0_sq[:, 0:256], in_=a_c[:], func=AF.Exp))
        aexp_ea = ea.n

        @block.vector
        def _(v):
            W(v, "v", ea, aexp_ea)
            ev.bump(v.tensor_scalar_mul(a_c[:], p0_sq[:, 0:256], -1.0))
        aneg_ev = ev.n

        if stage < 1:
            _finish(nc, block, W, gate, qs, ea, y_out, p7_o, AR2, stbf, take_snap)
            return nc

        ea_p0 = ea.n
        ev_p0 = ev.n
        ppsum = ExitStack()
        psTR = [ppsum.enter_context(nc.psum_tensor("psTR%d" % i, [128, 128], BF16))
                for i in range(2)]
        EA_TB = 19
        EV_TB = 4

        def p0_ea_v(tb, off):
            return ea_p0 + EA_TB * tb + off

        def p0_ev_v(tb, off):
            return ev_p0 + EV_TB * tb + off

        p0_mm = {}
        p0_ld = []

        @block.sync
        def _(sy):
            for tb in range(TB):
                if tb >= 2:
                    W(sy, "sy", ea, p0_ea_v(tb - 2, 3))
                    W(sy, "sy", ev, p0_ev_v(tb - 2, 1))
                q = qa[tb % 2]
                sy.dma_start(out=p0_x[tb % 2][:],
                             in_=x_in[tb * 128:(tb + 1) * 128, :]).then_inc(q.h, 16)
                q.n += 16
                p0_ld.append((q, q.n))

        for tb in range(TB):
            ld_q, ld_v = p0_ld[tb]

            @block.vector
            def _(v, tb=tb, ld_q=ld_q, ld_v=ld_v):
                W(v, "v", ld_q, ld_v)
                v_ = sev.bump(v.reduce_sum(smt[:, 0:1], p0_x[tb % 2][:],
                                           axis=AX.X))
                W(v, "v", sev, v_)
                sev.bump(v.tensor_scalar_mul(mean_c[:, tb:tb + 1], smt[:, 0:1],
                                             1.0 / H))
                ev.bump(v.tensor_copy(smt[:, 12:13], smt[:, 0:1]))
                assert ev.n == p0_ev_v(tb, 1)

            @block.scalar
            def _(sc, tb=tb, ld_q=ld_q, ld_v=ld_v):
                W(sc, "sc", ld_q, ld_v)
                if tb == 0:
                    W(sc, "sc", ev, aneg_ev)
                else:
                    W(sc, "sc", ev, p0_ev_v(tb - 1, 4))
                v_ = sea.bump(sc.activation(out=p0_sq[:], in_=p0_x[tb % 2][:],
                                            func=AF.Square, accum_out=smt[:, 1:2]))
                W(sc, "sc", sea, v_)
                ea.bump(sc.activation(out=smt[:, 2:3], in_=smt[:, 1:2],
                                      func=AF.Copy, scale=1.0 / H))
                assert ea.n == p0_ea_v(tb, 1)

            @block.vector
            def _(v, tb=tb):
                W(v, "v", ea, p0_ea_v(tb, 1))
                W(v, "v", sev, sev.n)
                ev.bump(v.scalar_tensor_tensor(smt[:, 3:4], mean_c[:, tb:tb + 1],
                                               mean_c[:, tb:tb + 1], smt[:, 2:3],
                                               op0=ALU.mult, op1=ALU.subtract))
                assert ev.n == p0_ev_v(tb, 2)

            @block.scalar
            def _(sc, tb=tb):
                W(sc, "sc", ev, p0_ev_v(tb, 2))
                v_ = sea.bump(sc.activation(out=smt[:, 6:7], in_=smt[:, 3:4],
                                            func=AF.Ln, scale=-1.0, bias=EPS))
                W(sc, "sc", sea, v_)
                ea.bump(sc.activation(out=rstd_c[:, tb:tb + 1], in_=smt[:, 6:7],
                                      func=AF.Exp, scale=-0.5))
                assert ea.n == p0_ea_v(tb, 2)

            @block.vector
            def _(v, tb=tb):
                W(v, "v", ea, p0_ea_v(tb, 2))
                W(v, "v", sev, sev.n)
                ev.bump(v.scalar_tensor_tensor(nmr_c[:, tb:tb + 1],
                                               mean_c[:, tb:tb + 1], -1.0,
                                               rstd_c[:, tb:tb + 1],
                                               op0=ALU.mult, op1=ALU.mult))
                assert ev.n == p0_ev_v(tb, 3)

            @block.scalar
            def _(sc, tb=tb):
                W(sc, "sc", ev, p0_ev_v(tb, 3))
                v_ = sea.bump(sc.activation(out=p0_xn[:], in_=p0_x[tb % 2][:],
                                            func=AF.Identity,
                                            scale=rstd_c[:, tb:tb + 1],
                                            bias=nmr_c[:, tb:tb + 1]))
                W(sc, "sc", sea, v_)
                v_ = sea.bump(sc.activation(out=p0_sq[:], in_=p0_xn[:],
                                            func=AF.Square,
                                            accum_out=smt[:, 9:10]))
                W(sc, "sc", sea, v_)
                v_ = sea.bump(sc.activation(out=smt[:, 10:11], in_=smt[:, 9:10],
                                            func=AF.Ln, scale=1.0 / H, bias=EPS))
                W(sc, "sc", sea, v_)
                ea.bump(sc.activation(out=smt[:, 11:12], in_=smt[:, 10:11],
                                      func=AF.Exp, scale=-0.5))
                assert ea.n == p0_ea_v(tb, 3)

            @block.vector
            def _(v, tb=tb):
                W(v, "v", ea, p0_ea_v(tb, 3))
                if tb >= 1:
                    W(v, "v", mm, p0_mm.get((tb - 1, HB - 1), 0))
                ev.bump(v.tensor_scalar_mul(p0_rtm[:], p0_xn[:], smt[:, 11:12]))
                assert ev.n == p0_ev_v(tb, 4)

            @block.tensor
            def _(te, tb=tb):
                if tb == 0:
                    W(te, "te", qc, const_qx)
                W(te, "te", ev, p0_ev_v(tb, 4))
                for hb in range(HB):
                    if tb > 0 or hb >= 2:
                        ptb, phb = (tb, hb - 2) if hb >= 2 else (tb - 1, HB - 2 + hb)
                        W(te, "te", ea, p0_ea_v(ptb, 4 + phb))
                    mm.bump(te.transpose(psTR[hb % 2][:],
                                         p0_rtm[:, hb * 128:(hb + 1) * 128],
                                         ident[:]))
                    p0_mm[(tb, hb)] = mm.n

            @block.scalar
            def _(sc, tb=tb):
                for hb in range(HB):
                    W(sc, "sc", mm, p0_mm[(tb, hb)])
                    ea.bump(sc.copy(
                        out=AR1[:, hb * 2048 + tb * 128: hb * 2048 + tb * 128 + 128],
                        in_=psTR[hb % 2][:]))
                    assert ea.n == p0_ea_v(tb, 4 + hb)

        if "rmsn" in taps:
            @block.sync
            def _(sy):
                W(sy, "sy", ea, p0_ea_v(TB - 1, 19))
                sy.dma_start(out=tap_t["rmsn"], in_=AR1[:]).then_inc(qs[3].h, 16)
                qs[3].n += 16

        take_snap()
        psP = [ppsum.enter_context(nc.psum_tensor("psP%d" % i, [128, 512], F32))
               for i in range(2)]
        mm_p1 = mm.n
        ea_p1 = ea.n
        qw_p1 = [qw_[0].n, qw_[1].n]
        qs_p1 = [q.n for q in qs]

        def p1_mm_tr(ob, hb):
            return mm_p1 + 20 * ob + hb + 1

        def p1_mm_ch(ob, t5):
            return mm_p1 + 20 * ob + 16 + t5 + 1

        def p1_ea_tr(ob, hb):
            return ea_p1 + 20 * ob + hb + 1

        def p1_ea_ev(ob, t5):
            return ea_p1 + 20 * ob + 16 + t5 + 1

        @block.gpsimd
        def _(gp):
            gate(gp, "gp")
            for ob in range(32):
                if ob >= 2:
                    W(gp, "gp", mm, p1_mm_tr(ob - 2, HB - 1))
                q = qw_[ob % 2]
                gp.dma_start(out=p1_row[ob % 2][:],
                             in_=ipw[ob * 128:(ob + 1) * 128, :]).then_inc(q.h, 16)
                q.n += 16

        @block.tensor
        def _(te):
            gate(te, "te")
            for ob in range(32):
                W(te, "te", qw_[ob % 2], qw_p1[ob % 2] + 16 * (ob // 2 + 1))
                for hb in range(HB):
                    gi = 16 * ob + hb
                    if gi >= 2:
                        pob, phb = divmod(gi - 2, HB)
                        W(te, "te", ea, p1_ea_tr(pob, phb) if pob * 20 >= 0 else 0)
                    mm.bump(te.transpose(psTR[hb % 2][:],
                                         p1_row[ob % 2][:, hb * 128:(hb + 1) * 128],
                                         ident[:]))
                    assert mm.n == p1_mm_tr(ob, hb)
                W(te, "te", ea, p1_ea_tr(ob, HB - 1))
                for t5 in range(NT5):
                    ci = 4 * ob + t5
                    if ci >= 2:
                        pob, pt5 = divmod(ci - 2, NT5)
                        W(te, "te", ea, p1_ea_ev(pob, pt5))
                    for hb in range(HB):
                        inst = te.matmul(
                            psP[ci % 2][:],
                            p1_ipt[ob % 2][:, hb * 128:(hb + 1) * 128],
                            AR1[:, hb * 2048 + t5 * 512: hb * 2048 + t5 * 512 + 512],
                            start=(hb == 0), stop=(hb == HB - 1))
                    mm.bump(inst)
                    assert mm.n == p1_mm_ch(ob, t5)

        @block.scalar
        def _(sc):
            gate(sc, "sc")
            for ob in range(32):
                if ob >= 2:
                    W(sc, "sc", mm, p1_mm_ch(ob - 2, NT5 - 1))
                for hb in range(HB):
                    W(sc, "sc", mm, p1_mm_tr(ob, hb))
                    ea.bump(sc.activation(
                        out=p1_ipt[ob % 2][:, hb * 128:(hb + 1) * 128],
                        in_=psTR[hb % 2][:], func=AF.Identity,
                        scale=w_c[:, hb:hb + 1]))
                    assert ea.n == p1_ea_tr(ob, hb)
                for t5 in range(NT5):
                    ci = 4 * ob + t5
                    W(sc, "sc", mm, p1_mm_ch(ob, t5))
                    if ob < HB:
                        inst = sc.copy(
                            out=AR2[:, ob * 2048 + t5 * 512:
                                    ob * 2048 + t5 * 512 + 512],
                            in_=psP[ci % 2][:])
                    else:
                        zi = ci - 64
                        if zi >= 4:
                            W(sc, "sc", qs[zi % 4],
                              qs_p1[zi % 4] + 16 * ((zi - 4) // 4 + 1))
                        slot = zi % 4
                        inst = sc.copy(out=stbf[:, slot * 512:(slot + 1) * 512],
                                       in_=psP[ci % 2][:])
                    ea.bump(inst)
                    assert ea.n == p1_ea_ev(ob, t5)

        @block.sync
        def _(sy):
            for ob in range(HB, 32):
                for t5 in range(NT5):
                    zi = 4 * (ob - HB) + t5
                    W(sy, "sy", ea, p1_ea_ev(ob, t5))
                    slot = zi % 4
                    q = qs[slot]
                    sy.dma_start(
                        out=z_sp[(ob - HB) * 128:(ob - HB + 1) * 128,
                                 t5 * 512:(t5 + 1) * 512],
                        in_=stbf[:, slot * 512:(slot + 1) * 512]).then_inc(q.h, 16)
                    q.n += 16
                    assert q.n == qs_p1[slot] + 16 * (zi // 4 + 1)

        if "c1" in taps:
            @block.sync
            def _(sy):
                W(sy, "sy", ea, p1_ea_ev(HB - 1, NT5 - 1))
                sy.dma_start(out=tap_t["c1"], in_=AR2[:]).then_inc(qs[3].h, 16)
                qs[3].n += 16
        if stage < 2:
            _finish(nc, block, W, gate, qs, ea, y_out, p7_o, AR2, stbf, take_snap)
            return nc

        take_snap()
        ppsum.close()
        ppsum = ExitStack()
        psC2 = [ppsum.enter_context(nc.psum_tensor("psC2%d" % i, [128, 512], F32))
                for i in range(4)]
        mm_p2 = mm.n
        ea_p2 = ea.n
        qw_p2 = [qw_[0].n, qw_[1].n]

        def p2_mm(di):
            return mm_p2 + di + 1

        def p2_ea(di):
            return ea_p2 + di + 1

        @block.gpsimd
        def _(gp):
            gate(gp, "gp")
            for ob in range(HB):
                if ob >= 2:
                    W(gp, "gp", mm, p2_mm(4 * (ob - 1)))
                src = convw[:, :, ob * 128:(ob + 1) * 128].rearrange(
                    "k (ib p) o -> p k ib o", p=128)
                q = qw_[ob % 2]
                gp.dma_start(out=p2_cw[ob % 2][:].rearrange(
                    "p (k ib o) -> p k ib o", k=4, ib=16), in_=src).then_inc(q.h, 16)
                q.n += 16

        @block.tensor
        def _(te):
            gate(te, "te")
            for ob in range(HB):
                W(te, "te", qw_[ob % 2], qw_p2[ob % 2] + 16 * (ob // 2 + 1))
                for t5 in range(NT5):
                    di = 4 * ob + t5
                    if di >= 4:
                        W(te, "te", ea, p2_ea(di - 4))
                    bank = psC2[di % 4]
                    first = True
                    for k in (3, 2, 1, 0):
                        shift = k - 3
                        for ib in range(HB):
                            lhs = p2_cw[ob % 2][:, (k * 16 + ib) * 128:
                                                (k * 16 + ib) * 128 + 128]
                            t0 = t5 * 512 + shift
                            if t0 >= 0:
                                rhs = AR2[:, ib * 2048 + t0: ib * 2048 + t0 + 512]
                                outp = bank[:]
                            else:
                                cut = -t0
                                rhs = AR2[:, ib * 2048: ib * 2048 + 512 - cut]
                                outp = bank[:, cut:512]
                            inst = te.matmul(outp, lhs, rhs, start=first,
                                             stop=(k == 0 and ib == HB - 1))
                            first = False
                    mm.bump(inst)
                    assert mm.n == p2_mm(di)

        @block.scalar
        def _(sc):
            gate(sc, "sc")
            for di in range(4 * HB):
                ob, t5 = divmod(di, NT5)
                W(sc, "sc", mm, p2_mm(di))
                ea.bump(sc.activation(
                    out=AR1[:, ob * 2048 + t5 * 512: ob * 2048 + t5 * 512 + 512],
                    in_=psC2[di % 4][:], func=AF.Silu, bias=cb_c[:, ob:ob + 1]))
                assert ea.n == p2_ea(di)

        if "xc" in taps:
            @block.sync
            def _(sy):
                W(sy, "sy", ea, p2_ea(4 * HB - 1))
                sy.dma_start(out=tap_t["xc"], in_=AR1[:]).then_inc(qs[3].h, 16)
                qs[3].n += 16
        if stage < 3:
            _finish(nc, block, W, gate, qs, ea, y_out, p7_o, AR1, stbf, take_snap)
            return nc

        take_snap()
        ppsum.close()
        ppsum = ExitStack()
        psTR = [ppsum.enter_context(nc.psum_tensor("psTR3%d" % i, [128, 128], BF16))
                for i in range(2)]
        mm_pp = mm.n
        ea_pp = ea.n
        qw_pp = [qw_[0].n, qw_[1].n]

        @block.gpsimd
        def _(gp):
            gate(gp, "gp")
            for ob in range(HB):
                if ob >= 2:
                    W(gp, "gp", mm, mm_pp + 16 * (ob - 1))
                q = qw_[ob % 2]
                gp.dma_start(out=p3_oprow[ob % 2][:],
                             in_=opw[ob * 128:(ob + 1) * 128, :]).then_inc(q.h, 16)
                q.n += 16
            gp.dma_start(out=p3_xpr[:], in_=xpw).then_inc(qd[0].h, 16)
            qd[0].n += 16
            gp.dma_start(out=p3_dtr[:].rearrange("p (hb r) -> p hb r", r=64),
                         in_=dtw.rearrange("(hb p) r -> p hb r", p=128)
                         ).then_inc(qd[1].h, 16)
            qd[1].n += 16

        @block.tensor
        def _(te):
            gate(te, "te")
            for ob in range(HB):
                W(te, "te", qw_[ob % 2], qw_pp[ob % 2] + 16 * (ob // 2 + 1))
                for hb in range(HB):
                    ti = ob * HB + hb
                    if ti >= 2:
                        W(te, "te", ea, ea_pp + ti - 1)
                    mm.bump(te.transpose(psTR[ti % 2][:],
                                         p3_oprow[ob % 2][:, hb * 128:(hb + 1) * 128],
                                         ident[:]))
            W(te, "te", qd[0], qd[0].n)
            for hb in range(HB):
                ti = 256 + hb
                W(te, "te", ea, ea_pp + ti - 1)
                mm.bump(te.transpose(psTR[ti % 2][:, 0:96],
                                     p3_xpr[:, hb * 128:(hb + 1) * 128],
                                     ident[0:96, 0:96]))
            W(te, "te", qd[1], qd[1].n)
            for hb in range(HB):
                ti = 272 + hb
                W(te, "te", ea, ea_pp + ti - 1)
                mm.bump(te.transpose(psTR[ti % 2][0:64, :],
                                     p3_dtr[:, hb * 64:(hb + 1) * 64], ident[:]))

        @block.scalar
        def _(sc):
            gate(sc, "sc")
            for ob in range(HB):
                for hb in range(HB):
                    ti = ob * HB + hb
                    W(sc, "sc", mm, mm_pp + ti + 1)
                    ea.bump(sc.copy(
                        out=AR2[:, hb * 2048 + ob * 128: hb * 2048 + ob * 128 + 128],
                        in_=psTR[ti % 2][:]))
            for hb in range(HB):
                ti = 256 + hb
                W(sc, "sc", mm, mm_pp + ti + 1)
                ea.bump(sc.copy(out=xpwT[:, hb * 96:(hb + 1) * 96],
                                in_=psTR[ti % 2][:, 0:96]))
            for hb in range(HB):
                ti = 272 + hb
                W(sc, "sc", mm, mm_pp + ti + 1)
                ea.bump(sc.copy(out=dtwT[:, hb * 128:(hb + 1) * 128],
                                in_=psTR[ti % 2][0:64, :]))

        @block.vector
        def _(v):
            ev.bump(v.memset(stat[:], 0.0))

        take_snap()
        ppsum.close()
        ppsum = ExitStack()
        pe_ = ppsum.enter_context
        psA = pe_(nc.psum_tensor("psA", [64, 256], F32))
        psB = pe_(nc.psum_tensor("psB", [16, 256], F32))
        psC = pe_(nc.psum_tensor("psC", [16, 256], F32))
        psD3 = [pe_(nc.psum_tensor("psD3%d" % i, [128, 256], F32)) for i in range(2)]
        psBC = [pe_(nc.psum_tensor("psBC%d" % i, [128, 256], F32)) for i in range(2)]
        psO = pe_(nc.psum_tensor("psO", [128, 512], F32))

        mm3 = mm.n
        ea3 = ea.n
        ev3 = ev.n
        qa3 = qa[0].n
        qs3 = [qs[0].n, qs[1].n]
        MM_CH = 59
        EA_CH = 315
        EV_CH = 288

        def v_mm(c, off):
            return mm3 + MM_CH * c + off

        def v_ea(c, off):
            return ea3 + EA_CH * c + off

        def v_ev(c, off):
            return ev3 + EV_CH * c + off

        def v_dfin(c, hb):
            return v_ev(c, 18 * hb + 1)

        def v_scan(c, gl):
            hb, n = divmod(gl, NST)
            return v_ev(c, 18 * hb + 2 + n)

        def v_qs3(c, d):
            return qs3[d % 2] + 16 * (4 * c + d // 2 + 1)

        zrdy = {}

        @block.sync
        def _(sy):
            gate(sy, "sy")
            src = z_sp[:, 0:TC].rearrange("(hb p) t -> p hb t", p=128)
            sy.dma_start(out=p3_z[:].rearrange("p (hb t) -> p hb t", t=TC),
                         in_=src).then_inc(qa[0].h, 16)
            qa[0].n += 16
            zrdy[0] = qa[0].n

        for c in range(NCH):
            cT = c * TC

            @block.tensor
            def _(te, c=c):
                if c == 0:
                    gate(te, "te")
                if c > 0:
                    W(te, "te", ea, v_ea(c - 1, 3))
                for part, ps, lo, hi in ((1, psA, 0, 64), (2, psB, 64, 80),
                                         (3, psC, 80, 96)):
                    for hb in range(HB):
                        inst = te.matmul(
                            ps[:], xpwT[:, hb * 96 + lo: hb * 96 + hi],
                            AR1[:, hb * 2048 + cT: hb * 2048 + cT + TC],
                            start=(hb == 0), stop=(hb == HB - 1))
                    mm.bump(inst)
                    assert mm.n == v_mm(c, part)
                W(te, "te", ea, v_ea(c, 3))
                for n in range(NST):
                    for half in (0, 1):
                        if n >= 1:
                            W(te, "te", ea, v_ea(c, 4 + 2 * (n - 1) + half))
                        elif c > 0:
                            W(te, "te", ea, v_ea(c - 1, 4 + 30 + half))
                        mm.bump(te.matmul(psBC[half][:],
                                          sel[:, n * 128:(n + 1) * 128],
                                          bmcm[:, half * 256:half * 256 + TC],
                                          start=True, stop=True))
                        assert mm.n == v_mm(c, 4 + 2 * n + half)
                W(te, "te", ea, v_ea(c, 2))
                for hb in range(HB):
                    if hb >= 2:
                        W(te, "te", ea, v_ea(c, 36 + 17 * (hb - 2)))
                    elif c > 0:
                        W(te, "te", ea, v_ea(c - 1, 36 + 17 * (14 + hb)))
                    mm.bump(te.matmul(psD3[hb % 2][:],
                                      dtwT[:, hb * 128:(hb + 1) * 128],
                                      din[:], start=True, stop=True))
                    assert mm.n == v_mm(c, 36 + hb)

            @block.scalar
            def _(sc, c=c):
                if c == 0:
                    gate(sc, "sc")
                W(sc, "sc", qa[0], zrdy[c])
                if c > 0:
                    W(sc, "sc", ev, v_ev(c - 1, EV_CH))
                for hb in range(HB):
                    inst = sc.activation(out=p3_sz[:, hb * 256:(hb + 1) * 256],
                                         in_=p3_z[:, hb * 256:(hb + 1) * 256],
                                         func=AF.Silu)
                ea.bump(inst)
                assert ea.n == v_ea(c, 1)
                W(sc, "sc", mm, v_mm(c, 1))
                ea.bump(sc.copy(out=din[:], in_=psA[:]))
                W(sc, "sc", mm, v_mm(c, 3))
                sc.copy(out=bmcm[:, 0:TC], in_=psB[:])
                ea.bump(sc.copy(out=bmcm[:, 256:256 + TC], in_=psC[:]))
                assert ea.n == v_ea(c, 3)
                for n in range(NST):
                    for half in (0, 1):
                        W(sc, "sc", mm, v_mm(c, 4 + 2 * n + half))
                        ea.bump(sc.copy(
                            out=p3_bc[:, (2 * n + half) * 256:
                                      (2 * n + half) * 256 + TC],
                            in_=psBC[half][:]))
                        assert ea.n == v_ea(c, 4 + 2 * n + half)
                for hb in range(HB):
                    W(sc, "sc", mm, v_mm(c, 36 + hb))
                    gl = c * HB + hb
                    if gl >= 2:
                        pc, ph = divmod(gl - 2, HB)
                        W(sc, "sc", ev, v_dfin(pc, ph))
                    sc.activation(out=p3_xs[hb % 2][:], in_=psD3[hb % 2][:],
                                  func=AF.Identity, bias=dtb_c[:, hb:hb + 1])
                    v_ = sea.bump(sc.activation(out=p3_tl[hb % 2][:],
                                                in_=psD3[hb % 2][:],
                                                func=AF.Abs,
                                                bias=dtb_c[:, hb:hb + 1]))
                    W(sc, "sc", sea, v_)
                    v_ = sea.bump(sc.activation(out=p3_tl[hb % 2][:],
                                                in_=p3_tl[hb % 2][:],
                                                func=AF.Exp, scale=-1.0))
                    W(sc, "sc", sea, v_)
                    ea.bump(sc.activation(out=p3_tl[hb % 2][:],
                                          in_=p3_tl[hb % 2][:],
                                          func=AF.Ln, bias=1.0))
                    assert ea.n == v_ea(c, 36 + 17 * hb)
                    W(sc, "sc", ev, v_dfin(c, hb))
                    for n in range(NST):
                        gl2 = hb * NST + n
                        if gl2 >= 3:
                            W(sc, "sc", ev, v_scan(c, gl2 - 3))
                        ea.bump(sc.activation(
                            out=p3_a[gl2 % 3][:], in_=p3_dl[hb % 4][:],
                            func=AF.Exp,
                            scale=a_c[:, hb * 16 + n: hb * 16 + n + 1]))
                        assert ea.n == v_ea(c, 37 + 17 * hb + n)

            @block.vector
            def _(v, c=c):
                if c == 0:
                    gate(v, "v")
                if c > 0:
                    W(v, "v", ev, v_ev(c - 1, EV_CH))
                if c > 0:
                    W(v, "v", mm, v_mm(c - 1, 59))
                for hb in range(HB):
                    W(v, "v", ea, v_ea(c, 36 + 17 * hb))
                    gl = c * HB + hb
                    if gl >= 4:
                        pc, ph = divmod(gl - 4, HB)
                        W(v, "v", ea, v_ea(pc, 37 + 17 * ph + NST - 1))
                    ev.bump(v.scalar_tensor_tensor(p3_dl[hb % 4][:],
                                                   p3_xs[hb % 2][:], 0.0,
                                                   p3_tl[hb % 2][:],
                                                   op0=ALU.max, op1=ALU.add))
                    assert ev.n == v_dfin(c, hb)
                    v.tensor_mul(p3_du[:], p3_dl[hb % 4][:],
                                 AR1[:, hb * 2048 + cT: hb * 2048 + cT + TC])
                    for n in range(NST):
                        gl2 = hb * NST + n
                        W(v, "v", ea, v_ea(c, 37 + 17 * hb + n))
                        W(v, "v", ea, v_ea(c, 4 + 2 * n + 1))
                        hs_sl = p3_hs4[:, (n % 4) * 256:(n % 4) * 256 + TC]
                        v.tensor_mul(p3_y[:], p3_du[:],
                                     p3_bc[:, (2 * n) * 256:(2 * n) * 256 + TC])
                        init = 0.0 if c == 0 else \
                            stat[:, hb * 16 + n: hb * 16 + n + 1]
                        ev.bump(v.tensor_tensor_scan(hs_sl, p3_a[gl2 % 3][:],
                                                     p3_y[:], init,
                                                     op0=ALU.mult, op1=ALU.add))
                        assert ev.n == v_scan(c, gl2)
                        if n % 4 == 3:
                            v.tensor_copy(
                                stat[:, hb * 16 + n - 3: hb * 16 + n + 1]
                                .rearrange("p (n o) -> p n o", o=1),
                                p3_hs4[:].rearrange("p (n s) -> p n s", s=256)
                                [:, :, TC - 1:TC])
                        if n == 0:
                            v.tensor_mul(p3_ya[:], hs_sl,
                                         p3_bc[:, (2 * n + 1) * 256:
                                               (2 * n + 1) * 256 + TC])
                        else:
                            v.tensor_mul(p3_y2[:], hs_sl,
                                         p3_bc[:, (2 * n + 1) * 256:
                                               (2 * n + 1) * 256 + TC])
                            v.tensor_add(p3_ya[:], p3_ya[:], p3_y2[:])
                    v.scalar_tensor_tensor(p3_y[:],
                                           AR1[:, hb * 2048 + cT:
                                               hb * 2048 + cT + TC],
                                           d_c[:, hb:hb + 1],
                                           p3_ya[:], op0=ALU.mult, op1=ALU.add)
                    W(v, "v", ea, v_ea(c, 1))
                    ev.bump(v.tensor_mul(p3_yg[:, hb * 256:(hb + 1) * 256],
                                         p3_y[:],
                                         p3_sz[:, hb * 256:(hb + 1) * 256]))
                    assert ev.n == v_ev(c, 18 * hb + 18)

            @block.tensor
            def _(te, c=c):
                W(te, "te", ev, v_ev(c, EV_CH))
                for d in range(8):
                    toff, h5 = divmod(d, 4)
                    if d > 0:
                        W(te, "te", ea, v_ea(c, 308 + d - 1))
                    elif c > 0:
                        W(te, "te", ea, v_ea(c - 1, 315))
                    for ob in range(HB):
                        inst = te.matmul(
                            psO[:],
                            p3_yg[:, ob * 256 + toff * 128:
                                  ob * 256 + toff * 128 + 128],
                            AR2[:, ob * 2048 + h5 * 512: ob * 2048 + h5 * 512 + 512],
                            start=(ob == 0), stop=(ob == HB - 1))
                    mm.bump(inst)
                    assert mm.n == v_mm(c, 52 + d)

            @block.scalar
            def _(sc, c=c):
                for d in range(8):
                    W(sc, "sc", mm, v_mm(c, 52 + d))
                    gd = c * 8 + d
                    if gd >= 2:
                        pc, pd = divmod(gd - 2, 8)
                        W(sc, "sc", qs[pd % 2], v_qs3(pc, pd))
                    slot = d % 2
                    ea.bump(sc.copy(out=stbf[:, slot * 512:(slot + 1) * 512],
                                    in_=psO[:]))
                    assert ea.n == v_ea(c, 308 + d)

            @block.sync
            def _(sy, c=c):
                for d in range(8):
                    toff, h5 = divmod(d, 4)
                    W(sy, "sy", ea, v_ea(c, 308 + d))
                    slot = d % 2
                    q = qs[slot]
                    sy.dma_start(
                        out=mo_sp[cT + toff * 128: cT + toff * 128 + 128,
                                  h5 * 512:(h5 + 1) * 512],
                        in_=stbf[:, slot * 512:(slot + 1) * 512]).then_inc(q.h, 16)
                    q.n += 16
                    assert q.n == v_qs3(c, d)
                if c + 1 < NCH:
                    W(sy, "sy", ea, v_ea(c, 1))
                    src = z_sp[:, (c + 1) * TC:(c + 2) * TC].rearrange(
                        "(hb p) t -> p hb t", p=128)
                    sy.dma_start(out=p3_z[:].rearrange("p (hb t) -> p hb t", t=TC),
                                 in_=src).then_inc(qa[0].h, 16)
                    qa[0].n += 16
                    zrdy[c + 1] = qa[0].n

        if stage < 4:
            _finish(nc, block, W, gate, qs, ea, y_out, p7_o, AR1, stbf, take_snap)
            return nc

        take_snap()
        ppsum.close()
        ppsum = ExitStack()
        psTR = [ppsum.enter_context(nc.psum_tensor("psTR4%d" % i, [128, 128], BF16))
                for i in range(2)]
        psL = ppsum.enter_context(nc.psum_tensor("psL", [128, 32], F32))
        psS = ppsum.enter_context(nc.psum_tensor("psS", [128, 16], F32))
        ea4 = ea.n
        ev4 = ev.n
        EA4 = 18
        EV4 = 3

        def p4_ea(tb, off):
            return ea4 + EA4 * tb + off

        def p4_ev(tb, off):
            return ev4 + EV4 * tb + off

        p4_mm = {}

        p4_ld = []

        @block.sync
        def _(sy):
            gate(sy, "sy")
            for tb in range(TB):
                if tb >= 2:
                    W(sy, "sy", ev, p4_ev(tb - 2, 1))
                    W(sy, "sy", ea, p4_ea(tb - 2, 1))
                sy.dma_start(out=p4_x[tb % 2][:],
                             in_=x_in[tb * 128:(tb + 1) * 128, :]
                             ).then_inc(qa[tb % 2].h, 16)
                qa[tb % 2].n += 16
                sy.dma_start(out=p4_mo[tb % 2][:],
                             in_=mo_sp[tb * 128:(tb + 1) * 128, :]
                             ).then_inc(qb[tb % 2].h, 16)
                qb[tb % 2].n += 16
                p4_ld.append((qa[tb % 2].n, qb[tb % 2].n))

        for tb in range(TB):
            @block.scalar
            def _(sc, tb=tb):
                if tb == 0:
                    gate(sc, "sc")
                W(sc, "sc", qa[tb % 2], p4_ld[tb][0])
                if tb >= 1:
                    W(sc, "sc", ev, p4_ev(tb - 1, 1))
                    W(sc, "sc", mm, p4_mm.get((tb - 1, HB - 1), 0))
                v_ = sea.bump(sc.activation(out=p4_xn[:], in_=p4_x[tb % 2][:],
                                            func=AF.Identity,
                                            scale=rstd_c[:, tb:tb + 1],
                                            bias=nmr_c[:, tb:tb + 1]))
                W(sc, "sc", sea, v_)
                ea.bump(sc.copy(out=stbf[:], in_=p4_xn[:]))
                assert ea.n == p4_ea(tb, 1)

            @block.vector
            def _(v, tb=tb):
                if tb == 0:
                    gate(v, "v")
                W(v, "v", ea, p4_ea(tb, 1))
                W(v, "v", qb[tb % 2], p4_ld[tb][1])
                v.tensor_add(p4_t1[:], p4_x[tb % 2][:], p4_xn[:])
                ev.bump(v.tensor_add(AR2[:, tb * 2048:(tb + 1) * 2048],
                                     p4_t1[:], p4_mo[tb % 2][:]))
                assert ev.n == p4_ev(tb, 1)

            @block.tensor
            def _(te, tb=tb):
                if tb == 0:
                    gate(te, "te")
                W(te, "te", ea, p4_ea(tb, 1))
                for hb in range(HB):
                    if tb > 0 or hb >= 2:
                        ptb, phb = (tb, hb - 2) if hb >= 2 else (tb - 1, HB - 2 + hb)
                        W(te, "te", ea, p4_ea(ptb, 2 + phb))
                    mm.bump(te.transpose(psTR[hb % 2][:],
                                         stbf[:, hb * 128:(hb + 1) * 128],
                                         ident[:]))
                    p4_mm[(tb, hb)] = mm.n

            @block.scalar
            def _(sc, tb=tb):
                for hb in range(HB):
                    W(sc, "sc", mm, p4_mm[(tb, hb)])
                    ea.bump(sc.copy(
                        out=AR1[:, hb * 2048 + tb * 128: hb * 2048 + tb * 128 + 128],
                        in_=psTR[hb % 2][:]))
                    assert ea.n == p4_ea(tb, 2 + hb)

            @block.tensor
            def _(te, tb=tb):
                W(te, "te", ea, p4_ea(tb, 17))
                if tb >= 1:
                    W(te, "te", ea, p4_ea(tb - 1, 18))
                for hb in range(HB):
                    inst = te.matmul(psL[:],
                                     AR1[:, hb * 2048 + tb * 128:
                                         hb * 2048 + tb * 128 + 128],
                                     gateT[:, hb * 32:(hb + 1) * 32],
                                     start=(hb == 0), stop=(hb == HB - 1))
                mm.bump(inst)
                p4_mm[("lg", tb)] = mm.n
                for hb in range(HB):
                    inst = te.matmul(psS[:, tb:tb + 1],
                                     AR1[:, hb * 2048 + tb * 128:
                                         hb * 2048 + tb * 128 + 128],
                                     segT[:, hb:hb + 1],
                                     start=(hb == 0), stop=(hb == HB - 1))
                mm.bump(inst)
                p4_mm[("sg", tb)] = mm.n

            @block.vector
            def _(v, tb=tb):
                W(v, "v", mm, p4_mm[("lg", tb)])
                ev.bump(v.reduce_max(smt[:, 16:17], psL[:], axis=AX.X, negate=True))
                assert ev.n == p4_ev(tb, 2)

            @block.scalar
            def _(sc, tb=tb):
                W(sc, "sc", ev, p4_ev(tb, 2))
                ea.bump(sc.activation(out=smt[:, 32:64], in_=psL[:], func=AF.Exp,
                                      bias=smt[:, 16:17]))
                assert ea.n == p4_ea(tb, 18)

            @block.vector
            def _(v, tb=tb):
                W(v, "v", ea, p4_ea(tb, 18))
                ops = [
                    lambda: v.reduce_sum(smt[:, 17:18], smt[:, 32:64], axis=AX.X),
                    lambda: v.reciprocal(smt[:, 18:19], smt[:, 17:18]),
                    lambda: v.tensor_scalar_mul(smt[:, 32:64], smt[:, 32:64],
                                                smt[:, 18:19]),
                    lambda: v.max(out=mx8[:], in_=smt[:, 32:64]),
                    lambda: v.tensor_single_scalar(comb[:, tb * 32:(tb + 1) * 32],
                                                   smt[:, 32:64], mx8[:, 3:4],
                                                   op=ALU.is_ge),
                    lambda: v.tensor_mul(comb[:, tb * 32:(tb + 1) * 32],
                                         comb[:, tb * 32:(tb + 1) * 32],
                                         smt[:, 32:64]),
                    lambda: v.reduce_sum(smt[:, 19:20],
                                         comb[:, tb * 32:(tb + 1) * 32],
                                         axis=AX.X),
                    lambda: v.reciprocal(smt[:, 20:21], smt[:, 19:20]),
                ]
                for op in ops:
                    v_ = sev.bump(op())
                    W(v, "v", sev, v_)
                ev.bump(v.tensor_scalar_mul(comb[:, tb * 32:(tb + 1) * 32],
                                            comb[:, tb * 32:(tb + 1) * 32],
                                            smt[:, 20:21]))
                assert ev.n == p4_ev(tb, 3)

        @block.scalar
        def _(sc):
            W(sc, "sc", mm, p4_mm[("sg", TB - 1)])
            ea.bump(sc.activation(out=s_c[:], in_=psS[:], func=AF.Sigmoid))

        if "comb" in taps:
            @block.vector
            def _(v):
                ev.bump(v.tensor_copy(AR1[:, 0:512], comb[:]))

            @block.sync
            def _(sy):
                W(sy, "sy", ev)
                sy.dma_start(out=tap_t["comb"][:, 0:512],
                             in_=AR1[:, 0:512]).then_inc(qs[3].h, 16)
                qs[3].n += 16
        if stage < 5:
            _finish(nc, block, W, gate, qs, ea, y_out, p7_o, AR2, stbf, take_snap)
            return nc

        take_snap()
        ppsum.close()
        ppsum = ExitStack()
        psG = [ppsum.enter_context(nc.psum_tensor("psG%d" % i, [128, 512], F32))
               for i in range(2)]
        psU = [ppsum.enter_context(nc.psum_tensor("psU%d" % i, [128, 512], F32))
               for i in range(2)]
        psD5 = [ppsum.enter_context(nc.psum_tensor("psD5%d" % i, [128, 512], F32))
                for i in range(4)]

        st5 = {"mm_wl": {}, "mm_dl": {}}

        def expert_phase(wgw, wuw, wdw, nfb, wgt_c, wut_c, wdt_b, hbuf, combcol):
            mmE = mm.n
            eaE = ea.n
            evE = ev.n
            qgE = [qg[0].n, qg[1].n]
            quE = [qu[0].n, qu[1].n]
            qdE = [qd[0].n, qd[1].n]
            nup = nfb * NT5

            def m_g(ui):
                return mmE + 2 * ui + 1

            def m_u(ui):
                return mmE + 2 * ui + 2

            def m_d(di):
                return mmE + 2 * nup + di + 1

            def a_sg(ui):
                return eaE + ui + 1

            def v_he(ui):
                return evE + ui + 1

            def v_de(di):
                return evE + nup + di + 1

            @block.gpsimd
            def _(gp):
                for f in range(nfb):
                    pv = st5["mm_wl"].get(f % 2, 0)
                    if pv:
                        W(gp, "gp", mm, pv)
                    src = wgw.rearrange("(hb p) (fb f) -> fb hb p f",
                                        p=128, f=128)[f].transpose([1, 0, 2])
                    gp.dma_start(out=wgt_c[f % 2][:].rearrange(
                        "p (hb f) -> p hb f", f=128), in_=src
                        ).then_inc(qg[f % 2].h, 16)
                    qg[f % 2].n += 16
                    src = wuw.rearrange("(hb p) (fb f) -> fb hb p f",
                                        p=128, f=128)[f].transpose([1, 0, 2])
                    gp.dma_start(out=wut_c[f % 2][:].rearrange(
                        "p (hb f) -> p hb f", f=128), in_=src
                        ).then_inc(qu[f % 2].h, 16)
                    qu[f % 2].n += 16
                    st5["mm_wl"][f % 2] = m_u((f + 1) * NT5 - 1)
                for h5 in range(4):
                    pv = st5["mm_dl"].get(h5 % 2, 0)
                    if pv:
                        W(gp, "gp", mm, pv)
                    src = wdw[:, h5 * 512:(h5 + 1) * 512].rearrange(
                        "(fb p) h -> p fb h", p=128)
                    gp.dma_start(out=wdt_b[h5 % 2][:, 0:nfb * 512].rearrange(
                        "p (fb h) -> p fb h", h=512), in_=src
                        ).then_inc(qd[h5 % 2].h, 16)
                    qd[h5 % 2].n += 16
                    st5["mm_dl"][h5 % 2] = m_d((h5 + 1) * TB - 1)

            @block.tensor
            def _(te):
                for f in range(nfb):
                    W(te, "te", qg[f % 2], qgE[f % 2] + 16 * (f // 2 + 1))
                    W(te, "te", qu[f % 2], quE[f % 2] + 16 * (f // 2 + 1))
                    for t5 in range(NT5):
                        ui = f * NT5 + t5
                        if ui >= 2:
                            W(te, "te", ea, a_sg(ui - 2))
                            W(te, "te", ev, v_he(ui - 2))
                        for hb in range(HB):
                            inst = te.matmul(
                                psG[ui % 2][:],
                                wgt_c[f % 2][:, hb * 128:(hb + 1) * 128],
                                AR1[:, hb * 2048 + t5 * 512:
                                    hb * 2048 + t5 * 512 + 512],
                                start=(hb == 0), stop=(hb == HB - 1))
                        mm.bump(inst)
                        assert mm.n == m_g(ui)
                        for hb in range(HB):
                            inst = te.matmul(
                                psU[ui % 2][:],
                                wut_c[f % 2][:, hb * 128:(hb + 1) * 128],
                                AR1[:, hb * 2048 + t5 * 512:
                                    hb * 2048 + t5 * 512 + 512],
                                start=(hb == 0), stop=(hb == HB - 1))
                        mm.bump(inst)
                        assert mm.n == m_u(ui)
                W(te, "te", ev, v_he(nup - 1))
                for h5 in range(4):
                    W(te, "te", qd[h5 % 2], qdE[h5 % 2] + 16 * (h5 // 2 + 1))
                    for tb in range(TB):
                        di = h5 * TB + tb
                        if di >= 4:
                            W(te, "te", ev, v_de(di - 4))
                        for fbk in range(nfb):
                            inst = te.matmul(
                                psD5[di % 4][:],
                                hbuf[:, fbk * 2048 + tb * 128:
                                     fbk * 2048 + tb * 128 + 128],
                                wdt_b[h5 % 2][:, fbk * 512:(fbk + 1) * 512],
                                start=(fbk == 0), stop=(fbk == nfb - 1))
                        mm.bump(inst)
                        assert mm.n == m_d(di)

            @block.scalar
            def _(sc):
                for ui in range(nup):
                    W(sc, "sc", mm, m_g(ui))
                    if ui >= 2:
                        W(sc, "sc", ev, v_he(ui - 2))
                    ea.bump(sc.activation(out=p5_sg[ui % 2][:], in_=psG[ui % 2][:],
                                          func=AF.Silu))
                    assert ea.n == a_sg(ui)

            @block.vector
            def _(v):
                W(v, "v", ev, evE)
                for ui in range(nup):
                    f, t5 = divmod(ui, NT5)
                    W(v, "v", ea, a_sg(ui))
                    W(v, "v", mm, m_u(ui))
                    ev.bump(v.tensor_mul(
                        hbuf[:, f * 2048 + t5 * 512: f * 2048 + t5 * 512 + 512],
                        p5_sg[ui % 2][:], psU[ui % 2][:]))
                    assert ev.n == v_he(ui)
                for di in range(4 * TB):
                    h5, tb = divmod(di, TB)
                    W(v, "v", mm, m_d(di))
                    accs = AR2[:, tb * 2048 + h5 * 512: tb * 2048 + h5 * 512 + 512]
                    ev.bump(v.scalar_tensor_tensor(accs, psD5[di % 4][:],
                                                   combcol(tb), accs,
                                                   op0=ALU.mult, op1=ALU.add))
                    assert ev.n == v_de(di)

        @block.gpsimd
        def _(gp):
            gate(gp, "gp")

        @block.tensor
        def _(te):
            gate(te, "te")

        for ee in range(NEXP):
            expert_phase(wg_w[ee], wu_w[ee], wd_w[ee], 6,
                         p5_wg, p5_wu, p5_wd, p5_he,
                         lambda tb, ee=ee: comb[:, tb * 32 + ee: tb * 32 + ee + 1])

        if stage >= 6:
            take_snap()
            @block.gpsimd
            def _(gp):
                gate(gp, "gp")

            @block.tensor
            def _(te):
                gate(te, "te")
            st5["mm_wl"] = {}
            st5["mm_dl"] = {}
            for fc in range(SF // 512):
                expert_phase(sg_w[:, fc * 512:(fc + 1) * 512],
                             su_w[:, fc * 512:(fc + 1) * 512],
                             sd_w[fc * 512:(fc + 1) * 512, :], 4,
                             p6_wg, p6_wu, p6_wd, p6_hs,
                             lambda tb: s_c[:, tb:tb + 1])

        _finish(nc, block, W, gate, qs, ea, y_out, p7_o, AR2, stbf, take_snap)
    return nc


def _finish(nc, block, W, gate, qs, ea, y_out, p7_o, accbuf, stbf, take_snap):
    take_snap()
    fin = {}
    qsF = [qs[0].n, qs[1].n]

    @block.scalar
    def _(sc):
        gate(sc, "sc")
        for tb in range(TB):
            if tb >= 2:
                W(sc, "sc", qs[tb % 2], qsF[tb % 2] + 16 * ((tb - 2) // 2 + 1))
            ea.bump(sc.copy(out=p7_o[tb % 2][:],
                            in_=accbuf[:, tb * 2048:(tb + 1) * 2048]))
            fin[("e", tb)] = ea.n

    @block.sync
    def _(sy):
        gate(sy, "sy")
        for tb in range(TB):
            W(sy, "sy", ea, fin[("e", tb)])
            q = qs[tb % 2]
            sy.dma_start(out=y_out[tb * 128:(tb + 1) * 128, :],
                         in_=p7_o[tb % 2][:]).then_inc(q.h, 16)
            q.n += 16
        sy.wait_ge(qs[0].h, qs[0].n)
        sy.wait_ge(qs[1].h, qs[1].n)


def kernel(**inputs):
    x = np.ascontiguousarray(np.asarray(inputs["L0_attn_x"], dtype=np.float32))
    shared = {
        "norm_w": inputs["L0_attn_norm_w"],
        "in_proj_w": inputs["L0_attn_in_proj_w"],
        "conv_w": inputs["L0_attn_conv_w"],
        "conv_b": inputs["L0_attn_conv_b"],
        "x_proj_w": inputs["L0_attn_x_proj_w"],
        "dt_proj_w": inputs["L0_attn_dt_proj_w"],
        "dt_proj_b": inputs["L0_attn_dt_proj_b"],
        "A_log": inputs["L0_attn_A_log"],
        "D": inputs["L0_attn_D"],
        "out_proj_w": inputs["L0_attn_out_proj_w"],
        "gate_w": inputs["L0_ffn_gate_w"],
        "wg": inputs["L0_ffn_wg"],
        "wu": inputs["L0_ffn_wu"],
        "wd": inputs["L0_ffn_wd"],
        "sg": inputs["L0_ffn_sg"],
        "su": inputs["L0_ffn_su"],
        "sd": inputs["L0_ffn_sd"],
        "seg": inputs["L0_ffn_seg"],
    }
    shared = {k: np.ascontiguousarray(np.asarray(v, dtype=np.float32))
              for k, v in shared.items()}
    nc = build()
    in_maps = [dict(x=x[c], **shared) for c in range(8)]
    res = run_bass_kernel_spmd(nc, in_maps, core_ids=list(range(8)))
    out = np.stack([np.asarray(res.results[c]["y"]) for c in range(8)], axis=0)
    return out.astype(np.float32)


# version 09
# speedup vs baseline: 1.0857x; 1.0857x over previous
import numpy as np
import ml_dtypes
from contextlib import ExitStack

import concourse.bass as bass
import concourse.mybir as mybir
from concourse.bass_utils import run_bass_kernel_spmd

F32 = mybir.dt.float32
BF16 = mybir.dt.bfloat16
AF = mybir.ActivationFunctionType
ALU = mybir.AluOpType
AX = mybir.AxisListType

L = 2048
H = 2048
HB = 16
TB = 16
TC = 256
NCH = L // TC
DTR = 64
NST = 16
FEXP = 768
NEXP = 32
SF = 5632
EPS = 1e-5
NT5 = L // 512
DEDUP = True


class Sem:
    def __init__(self, ctx, nc, name):
        self.h = ctx.enter_context(nc.semaphore(name))
        self.n = 0

    def bump(self, inst, k=1):
        inst.then_inc(self.h, k)
        self.n += k
        return self.n


def build(stage=7, taps=()):
    taps = set(taps)
    nc = bass.Bass("TRN2", target_bir_lowering=False)

    def I(name, shape):
        return nc.dram_tensor(name, shape, F32, kind="ExternalInput").ap()

    x_in = I("x", [L, H])
    norm_w = I("norm_w", [H])
    ipw = I("in_proj_w", [2 * H, H])
    convw = I("conv_w", [4, H, H])
    convb = I("conv_b", [H])
    xpw = I("x_proj_w", [96, H])
    dtw = I("dt_proj_w", [H, DTR])
    dtb = I("dt_proj_b", [H])
    alog = I("A_log", [H, NST])
    dvec = I("D", [H])
    opw = I("out_proj_w", [H, H])
    gw = I("gate_w", [H, NEXP])
    wg_w = I("wg", [NEXP, H, FEXP])
    wu_w = I("wu", [NEXP, H, FEXP])
    wd_w = I("wd", [NEXP, FEXP, H])
    sg_w = I("sg", [H, SF])
    su_w = I("su", [H, SF])
    sd_w = I("sd", [SF, H])
    seg_w = I("seg", [H, 1])

    y_out = nc.dram_tensor("y", [L, H], F32, kind="ExternalOutput").ap()
    zk = "ExternalOutput" if "z" in taps else "Internal"
    mk = "ExternalOutput" if "mo" in taps else "Internal"
    z_sp = nc.dram_tensor("z_sp", [H, L], BF16, kind=zk).ap()
    mo_sp = nc.dram_tensor("mo_sp", [L, H], BF16, kind=mk).ap()
    tap_t = {}
    for t in taps - {"z", "mo"}:
        tap_t[t] = nc.dram_tensor("tap_" + t, [128, HB * 2048], BF16,
                                  kind="ExternalOutput").ap()

    ident_d = nc.inline_tensor(np.eye(128, dtype=ml_dtypes.bfloat16), "identc").ap()
    CONSTV = [EPS, 1.0, -1.0, -0.5, 1.0 / H, 0.0, 2.0, 0.5]
    cons_d = nc.inline_tensor(
        np.tile(np.array(CONSTV, np.float32), (128, 1)), "consv").ap()
    sel_np = np.zeros((16, 16 * 128), dtype=ml_dtypes.bfloat16)
    for n in range(16):
        sel_np[n, n * 128:(n + 1) * 128] = 1
    sel_d = nc.inline_tensor(sel_np, "selc").ap()

    with ExitStack() as ctx:
        e = ctx.enter_context
        AR1 = nc.alloc_sbuf_tensor("AR1", [128, HB * 2048], BF16)
        AR2 = nc.alloc_sbuf_tensor("AR2", [128, HB * 2048], BF16)
        ident = nc.alloc_sbuf_tensor("ident", [128, 128], BF16)
        sel = nc.alloc_sbuf_tensor("sel", [16, 2048], BF16)
        mean_c = nc.alloc_sbuf_tensor("mean_c", [128, 16], F32)
        rstd_c = nc.alloc_sbuf_tensor("rstd_c", [128, 16], F32)
        nmr_c = nc.alloc_sbuf_tensor("nmr_c", [128, 16], F32)
        w_c = nc.alloc_sbuf_tensor("w_c", [128, 16], F32)
        cb_c = nc.alloc_sbuf_tensor("cb_c", [128, 16], F32)
        dtb_c = nc.alloc_sbuf_tensor("dtb_c", [128, 16], F32)
        d_c = nc.alloc_sbuf_tensor("d_c", [128, 16], F32)
        s_c = nc.alloc_sbuf_tensor("s_c", [128, 16], F32)
        a_c = nc.alloc_sbuf_tensor("a_c", [128, 256], F32)
        comb = nc.alloc_sbuf_tensor("comb", [128, 512], F32)
        gateT = nc.alloc_sbuf_tensor("gateT", [128, 512], BF16)
        segT = nc.alloc_sbuf_tensor("segT", [128, 16], BF16)
        xpwT = nc.alloc_sbuf_tensor("xpwT", [128, 16 * 96], BF16)
        dtwT = nc.alloc_sbuf_tensor("dtwT", [64, 2048], BF16)
        bmcm = nc.alloc_sbuf_tensor("bmcm", [16, 512], BF16)
        din = nc.alloc_sbuf_tensor("din", [64, 256], BF16)
        stat = nc.alloc_sbuf_tensor("stat", [128, 256], F32)
        stbf = nc.alloc_sbuf_tensor("stbf", [128, 2048], BF16)
        smt = nc.alloc_sbuf_tensor("smt", [128, 64], F32)
        mx8 = nc.alloc_sbuf_tensor("mx8", [128, 8], F32)
        cons = nc.alloc_sbuf_tensor("cons", [128, 8], F32)
        for ci_, cv_ in enumerate(CONSTV):
            nc.const_aps.register(F32, cv_, cons[:, ci_:ci_ + 1])

        FLEX = 57344
        fl = nc.bump_sbuf(FLEX)
        assert fl is not None, "flex arena alloc failed"
        fb = fl[0]

        def flex(name, shape, dtype, off):
            return nc.alloc_sbuf_tensor_at(name, shape, dtype, offset=fb + off)

        p0_x = [flex("p0_x%d" % i, [128, 2048], F32, i * 8192) for i in range(2)]
        p0_xn = flex("p0_xn", [128, 2048], F32, 16384)
        p0_sq = flex("p0_sq", [128, 2048], F32, 24576)
        p0_rtm = flex("p0_rtm", [128, 2048], BF16, 32768)

        p1_row = [flex("p1_row%d" % i, [128, 2048], BF16, i * 4096) for i in range(2)]
        p1_ipt = [flex("p1_ipt%d" % i, [128, 2048], BF16, 8192 + i * 4096)
                  for i in range(2)]

        p2_cw = [flex("p2_cw%d" % i, [128, 8192], BF16, i * 16384) for i in range(2)]

        p3_oprow = [flex("p3_oprow%d" % i, [128, 2048], BF16, i * 4096)
                    for i in range(2)]
        p3_xpr = flex("p3_xpr", [96, 2048], BF16, 8192)
        p3_dtr = flex("p3_dtr", [128, 1024], BF16, 12288)

        p3_bc = flex("p3_bc", [128, 16 * 512], BF16, 0)
        p3_z = flex("p3_z", [128, 4096], BF16, 16384)
        p3_yg = flex("p3_yg", [128, 4096], BF16, 24576)
        p3_sz = flex("p3_sz", [128, 4096], BF16, 32768)
        p3_dl = [flex("p3_dl%d" % i, [128, 256], F32, 40960 + i * 1024)
                 for i in range(4)]
        p3_a = [flex("p3_a%d" % i, [128, 256], F32, 45056 + i * 1024)
                for i in range(3)]
        p3_du = flex("p3_du", [128, 256], F32, 48128)
        p3_xs = [flex("p3_xs%d" % i, [128, 256], F32, 49152 + i * 1024)
                 for i in range(2)]
        p3_tl = [flex("p3_tl%d" % i, [128, 256], F32, 51200 + i * 1024)
                 for i in range(2)]
        p3_hs4 = flex("p3_hs4", [128, 1024], BF16, 53248)
        p3_ya = flex("p3_ya", [128, 256], BF16, 55296)
        p3_y2 = flex("p3_y2", [128, 256], BF16, 55808)
        p3_y = flex("p3_y", [128, 256], F32, 56320)
        p3_t1 = flex("p3_t1", [128, 512], BF16, 56320)

        p4_x = [flex("p4_x%d" % i, [128, 2048], F32, i * 8192) for i in range(2)]
        p4_mo = [flex("p4_mo%d" % i, [128, 2048], BF16, 16384 + i * 4096)
                 for i in range(2)]
        p4_xn = flex("p4_xn", [128, 2048], F32, 24576)
        p4_t1 = flex("p4_t1", [128, 2048], F32, 32768)

        p5_he = flex("p5_he", [128, 6 * 2048], BF16, 0)
        p5_wg = [flex("p5_wg%d" % i, [128, 2048], BF16, 24576 + i * 4096)
                 for i in range(2)]
        p5_wu = [flex("p5_wu%d" % i, [128, 2048], BF16, 32768 + i * 4096)
                 for i in range(2)]
        p5_wd = [flex("p5_wd%d" % i, [128, 3072], BF16, 40960 + i * 6144)
                 for i in range(2)]
        p5_sg = [flex("p5_sg%d" % i, [128, 512], F32, 53248 + i * 2048)
                 for i in range(2)]

        p6_hs = flex("p6_hs", [128, 4 * 2048], BF16, 0)
        p6_wg = [flex("p6_wg%d" % i, [128, 2048], BF16, 16384 + i * 4096)
                 for i in range(2)]
        p6_wu = [flex("p6_wu%d" % i, [128, 2048], BF16, 24576 + i * 4096)
                 for i in range(2)]
        p6_wd = [flex("p6_wd%d" % i, [128, 2048], BF16, 32768 + i * 4096)
                 for i in range(2)]

        p7_o = [flex("p7_o%d" % i, [128, 2048], F32, i * 8192) for i in range(2)]

        qc = Sem(ctx, nc, "qc")
        qa = [Sem(ctx, nc, "qa%d" % i) for i in range(2)]
        qb = [Sem(ctx, nc, "qb%d" % i) for i in range(2)]
        qs = [Sem(ctx, nc, "qs%d" % i) for i in range(4)]
        qw_ = [Sem(ctx, nc, "qw%d" % i) for i in range(2)]
        qg = [Sem(ctx, nc, "qg%d" % i) for i in range(2)]
        qu = [Sem(ctx, nc, "qu%d" % i) for i in range(2)]
        qd = [Sem(ctx, nc, "qd%d" % i) for i in range(2)]
        mm = Sem(ctx, nc, "mm")
        ea = Sem(ctx, nc, "ea")
        ev = Sem(ctx, nc, "ev")
        sea = Sem(ctx, nc, "sea")
        sev = Sem(ctx, nc, "sev")
        SEMS = [qc] + qa + qb + qs + qw_ + qg + qu + qd + [mm, ea, ev]
        OWN = {"te": mm, "sc": ea, "v": ev}
        last_wait = {}
        snap = {}

        def take_snap():
            for s in SEMS:
                snap[s.h.name] = s.n

        def W(eng, tag, s, v=None):
            v = s.n if v is None else v
            if v is None or v <= 0:
                return
            k = (tag, s.h.name)
            if DEDUP and last_wait.get(k, -1) >= v:
                return
            last_wait[k] = v
            eng.wait_ge(s.h, v)

        def gate(eng, tag):
            for s in SEMS:
                if OWN.get(tag) is s:
                    continue
                W(eng, tag, s, snap[s.h.name])

        block = e(nc.Block())

        @block.sync
        def _(sy):
            sy.dma_start(out=ident[:], in_=ident_d).then_inc(qc.h, 16)
            sy.dma_start(out=cons[:], in_=cons_d).then_inc(qc.h, 16)
            sy.dma_start(out=sel[:], in_=sel_d).then_inc(qc.h, 16)
            for vec, dst in ((norm_w, w_c), (convb, cb_c), (dtb, dtb_c),
                             (dvec, d_c)):
                for hb in range(HB):
                    sy.dma_start(out=dst[:, hb:hb + 1],
                                 in_=vec[hb * 128:(hb + 1) * 128]
                                 ).then_inc(qc.h, 16)
            sy.dma_start(out=a_c[:].rearrange("p (hb n) -> p hb n", n=16),
                         in_=alog.rearrange("(hb p) n -> p hb n", p=128)
                         ).then_inc(qc.h, 16)
            qc.n += 16 * 68

        @block.gpsimd
        def _(gp):
            gp.dma_start(out=gateT[:].rearrange("p (hb e) -> p hb e", e=32),
                         in_=gw.rearrange("(hb p) e -> p hb e", p=128)
                         ).then_inc(qg[0].h, 16)
            qg[0].n += 16
            gp.dma_start(out=segT[:].rearrange("p (hb o) -> p hb o", o=1),
                         in_=seg_w.rearrange("(hb p) o -> p hb o", p=128),
                         allow_slow_non_contiguous=True).then_inc(qg[1].h, 16)
            qg[1].n += 16
        const_qx = qc.n

        @block.scalar
        def _(sc):
            W(sc, "sc", qc, const_qx)
            ea.bump(sc.activation(out=p0_sq[:, 0:256], in_=a_c[:], func=AF.Exp))
        aexp_ea = ea.n

        @block.vector
        def _(v):
            W(v, "v", ea, aexp_ea)
            ev.bump(v.tensor_scalar_mul(a_c[:], p0_sq[:, 0:256], -1.0))
        aneg_ev = ev.n

        if stage < 1:
            _finish(nc, block, W, gate, qs, ea, y_out, p7_o, AR2, stbf, take_snap)
            return nc

        ea_p0 = ea.n
        ev_p0 = ev.n
        ppsum = ExitStack()
        psTR = [ppsum.enter_context(nc.psum_tensor("psTR%d" % i, [128, 128], BF16))
                for i in range(2)]
        EA_TB = 19
        EV_TB = 4

        def p0_ea_v(tb, off):
            return ea_p0 + EA_TB * tb + off

        def p0_ev_v(tb, off):
            return ev_p0 + EV_TB * tb + off

        p0_mm = {}
        p0_ld = []

        @block.sync
        def _(sy):
            for tb in range(TB):
                if tb >= 2:
                    W(sy, "sy", ea, p0_ea_v(tb - 2, 3))
                    W(sy, "sy", ev, p0_ev_v(tb - 2, 1))
                q = qa[tb % 2]
                sy.dma_start(out=p0_x[tb % 2][:],
                             in_=x_in[tb * 128:(tb + 1) * 128, :]).then_inc(q.h, 16)
                q.n += 16
                p0_ld.append((q, q.n))

        for tb in range(TB):
            ld_q, ld_v = p0_ld[tb]

            @block.vector
            def _(v, tb=tb, ld_q=ld_q, ld_v=ld_v):
                W(v, "v", ld_q, ld_v)
                v_ = sev.bump(v.reduce_sum(smt[:, 0:1], p0_x[tb % 2][:],
                                           axis=AX.X))
                W(v, "v", sev, v_)
                sev.bump(v.tensor_scalar_mul(mean_c[:, tb:tb + 1], smt[:, 0:1],
                                             1.0 / H))
                ev.bump(v.tensor_copy(smt[:, 12:13], smt[:, 0:1]))
                assert ev.n == p0_ev_v(tb, 1)

            @block.scalar
            def _(sc, tb=tb, ld_q=ld_q, ld_v=ld_v):
                W(sc, "sc", ld_q, ld_v)
                if tb == 0:
                    W(sc, "sc", ev, aneg_ev)
                else:
                    W(sc, "sc", ev, p0_ev_v(tb - 1, 4))
                v_ = sea.bump(sc.activation(out=p0_sq[:], in_=p0_x[tb % 2][:],
                                            func=AF.Square, accum_out=smt[:, 1:2]))
                W(sc, "sc", sea, v_)
                ea.bump(sc.activation(out=smt[:, 2:3], in_=smt[:, 1:2],
                                      func=AF.Copy, scale=1.0 / H))
                assert ea.n == p0_ea_v(tb, 1)

            @block.vector
            def _(v, tb=tb):
                W(v, "v", ea, p0_ea_v(tb, 1))
                W(v, "v", sev, sev.n)
                ev.bump(v.scalar_tensor_tensor(smt[:, 3:4], mean_c[:, tb:tb + 1],
                                               mean_c[:, tb:tb + 1], smt[:, 2:3],
                                               op0=ALU.mult, op1=ALU.subtract))
                assert ev.n == p0_ev_v(tb, 2)

            @block.scalar
            def _(sc, tb=tb):
                W(sc, "sc", ev, p0_ev_v(tb, 2))
                v_ = sea.bump(sc.activation(out=smt[:, 6:7], in_=smt[:, 3:4],
                                            func=AF.Ln, scale=-1.0, bias=EPS))
                W(sc, "sc", sea, v_)
                ea.bump(sc.activation(out=rstd_c[:, tb:tb + 1], in_=smt[:, 6:7],
                                      func=AF.Exp, scale=-0.5))
                assert ea.n == p0_ea_v(tb, 2)

            @block.vector
            def _(v, tb=tb):
                W(v, "v", ea, p0_ea_v(tb, 2))
                W(v, "v", sev, sev.n)
                ev.bump(v.scalar_tensor_tensor(nmr_c[:, tb:tb + 1],
                                               mean_c[:, tb:tb + 1], -1.0,
                                               rstd_c[:, tb:tb + 1],
                                               op0=ALU.mult, op1=ALU.mult))
                assert ev.n == p0_ev_v(tb, 3)

            @block.scalar
            def _(sc, tb=tb):
                W(sc, "sc", ev, p0_ev_v(tb, 3))
                v_ = sea.bump(sc.activation(out=p0_xn[:], in_=p0_x[tb % 2][:],
                                            func=AF.Identity,
                                            scale=rstd_c[:, tb:tb + 1],
                                            bias=nmr_c[:, tb:tb + 1]))
                W(sc, "sc", sea, v_)
                v_ = sea.bump(sc.activation(out=p0_sq[:], in_=p0_xn[:],
                                            func=AF.Square,
                                            accum_out=smt[:, 9:10]))
                W(sc, "sc", sea, v_)
                v_ = sea.bump(sc.activation(out=smt[:, 10:11], in_=smt[:, 9:10],
                                            func=AF.Ln, scale=1.0 / H, bias=EPS))
                W(sc, "sc", sea, v_)
                ea.bump(sc.activation(out=smt[:, 11:12], in_=smt[:, 10:11],
                                      func=AF.Exp, scale=-0.5))
                assert ea.n == p0_ea_v(tb, 3)

            @block.vector
            def _(v, tb=tb):
                W(v, "v", ea, p0_ea_v(tb, 3))
                if tb >= 1:
                    W(v, "v", mm, p0_mm.get((tb - 1, HB - 1), 0))
                ev.bump(v.tensor_scalar_mul(p0_rtm[:], p0_xn[:], smt[:, 11:12]))
                assert ev.n == p0_ev_v(tb, 4)

            @block.tensor
            def _(te, tb=tb):
                if tb == 0:
                    W(te, "te", qc, const_qx)
                W(te, "te", ev, p0_ev_v(tb, 4))
                for hb in range(HB):
                    if tb > 0 or hb >= 2:
                        ptb, phb = (tb, hb - 2) if hb >= 2 else (tb - 1, HB - 2 + hb)
                        W(te, "te", ea, p0_ea_v(ptb, 4 + phb))
                    mm.bump(te.transpose(psTR[hb % 2][:],
                                         p0_rtm[:, hb * 128:(hb + 1) * 128],
                                         ident[:]))
                    p0_mm[(tb, hb)] = mm.n

            @block.scalar
            def _(sc, tb=tb):
                for hb in range(HB):
                    W(sc, "sc", mm, p0_mm[(tb, hb)])
                    ea.bump(sc.copy(
                        out=AR1[:, hb * 2048 + tb * 128: hb * 2048 + tb * 128 + 128],
                        in_=psTR[hb % 2][:]))
                    assert ea.n == p0_ea_v(tb, 4 + hb)

        if "rmsn" in taps:
            @block.sync
            def _(sy):
                W(sy, "sy", ea, p0_ea_v(TB - 1, 19))
                sy.dma_start(out=tap_t["rmsn"], in_=AR1[:]).then_inc(qs[3].h, 16)
                qs[3].n += 16

        take_snap()
        psP = [ppsum.enter_context(nc.psum_tensor("psP%d" % i, [128, 512], F32))
               for i in range(2)]
        mm_p1 = mm.n
        ea_p1 = ea.n
        qw_p1 = [qw_[0].n, qw_[1].n]
        qs_p1 = [q.n for q in qs]

        def p1_mm_tr(ob, hb):
            return mm_p1 + 20 * ob + hb + 1

        def p1_mm_ch(ob, t5):
            return mm_p1 + 20 * ob + 16 + t5 + 1

        def p1_ea_tr(ob, hb):
            return ea_p1 + 20 * ob + hb + 1

        def p1_ea_ev(ob, t5):
            return ea_p1 + 20 * ob + 16 + t5 + 1

        @block.gpsimd
        def _(gp):
            gate(gp, "gp")
            for ob in range(32):
                if ob >= 2:
                    W(gp, "gp", mm, p1_mm_tr(ob - 2, HB - 1))
                q = qw_[ob % 2]
                gp.dma_start(out=p1_row[ob % 2][:],
                             in_=ipw[ob * 128:(ob + 1) * 128, :]).then_inc(q.h, 16)
                q.n += 16

        @block.tensor
        def _(te):
            gate(te, "te")
            for ob in range(32):
                W(te, "te", qw_[ob % 2], qw_p1[ob % 2] + 16 * (ob // 2 + 1))
                for hb in range(HB):
                    gi = 16 * ob + hb
                    if gi >= 2:
                        pob, phb = divmod(gi - 2, HB)
                        W(te, "te", ea, p1_ea_tr(pob, phb) if pob * 20 >= 0 else 0)
                    mm.bump(te.transpose(psTR[hb % 2][:],
                                         p1_row[ob % 2][:, hb * 128:(hb + 1) * 128],
                                         ident[:]))
                    assert mm.n == p1_mm_tr(ob, hb)
                W(te, "te", ea, p1_ea_tr(ob, HB - 1))
                for t5 in range(NT5):
                    ci = 4 * ob + t5
                    if ci >= 2:
                        pob, pt5 = divmod(ci - 2, NT5)
                        W(te, "te", ea, p1_ea_ev(pob, pt5))
                    for hb in range(HB):
                        inst = te.matmul(
                            psP[ci % 2][:],
                            p1_ipt[ob % 2][:, hb * 128:(hb + 1) * 128],
                            AR1[:, hb * 2048 + t5 * 512: hb * 2048 + t5 * 512 + 512],
                            start=(hb == 0), stop=(hb == HB - 1))
                    mm.bump(inst)
                    assert mm.n == p1_mm_ch(ob, t5)

        @block.scalar
        def _(sc):
            gate(sc, "sc")
            for ob in range(32):
                if ob >= 2:
                    W(sc, "sc", mm, p1_mm_ch(ob - 2, NT5 - 1))
                for hb in range(HB):
                    W(sc, "sc", mm, p1_mm_tr(ob, hb))
                    ea.bump(sc.activation(
                        out=p1_ipt[ob % 2][:, hb * 128:(hb + 1) * 128],
                        in_=psTR[hb % 2][:], func=AF.Identity,
                        scale=w_c[:, hb:hb + 1]))
                    assert ea.n == p1_ea_tr(ob, hb)
                for t5 in range(NT5):
                    ci = 4 * ob + t5
                    W(sc, "sc", mm, p1_mm_ch(ob, t5))
                    if ob < HB:
                        inst = sc.copy(
                            out=AR2[:, ob * 2048 + t5 * 512:
                                    ob * 2048 + t5 * 512 + 512],
                            in_=psP[ci % 2][:])
                    else:
                        zi = ci - 64
                        if zi >= 4:
                            W(sc, "sc", qs[zi % 4],
                              qs_p1[zi % 4] + 16 * ((zi - 4) // 4 + 1))
                        slot = zi % 4
                        inst = sc.copy(out=stbf[:, slot * 512:(slot + 1) * 512],
                                       in_=psP[ci % 2][:])
                    ea.bump(inst)
                    assert ea.n == p1_ea_ev(ob, t5)

        @block.sync
        def _(sy):
            for ob in range(HB, 32):
                for t5 in range(NT5):
                    zi = 4 * (ob - HB) + t5
                    W(sy, "sy", ea, p1_ea_ev(ob, t5))
                    slot = zi % 4
                    q = qs[slot]
                    sy.dma_start(
                        out=z_sp[(ob - HB) * 128:(ob - HB + 1) * 128,
                                 t5 * 512:(t5 + 1) * 512],
                        in_=stbf[:, slot * 512:(slot + 1) * 512]).then_inc(q.h, 16)
                    q.n += 16
                    assert q.n == qs_p1[slot] + 16 * (zi // 4 + 1)

        if "c1" in taps:
            @block.sync
            def _(sy):
                W(sy, "sy", ea, p1_ea_ev(HB - 1, NT5 - 1))
                sy.dma_start(out=tap_t["c1"], in_=AR2[:]).then_inc(qs[3].h, 16)
                qs[3].n += 16
        if stage < 2:
            _finish(nc, block, W, gate, qs, ea, y_out, p7_o, AR2, stbf, take_snap)
            return nc

        take_snap()
        ppsum.close()
        ppsum = ExitStack()
        psC2 = [ppsum.enter_context(nc.psum_tensor("psC2%d" % i, [128, 512], F32))
                for i in range(4)]
        mm_p2 = mm.n
        ea_p2 = ea.n
        qw_p2 = [qw_[0].n, qw_[1].n]

        def p2_mm(di):
            return mm_p2 + di + 1

        def p2_ea(di):
            return ea_p2 + di + 1

        @block.gpsimd
        def _(gp):
            gate(gp, "gp")
            for ob in range(HB):
                if ob >= 2:
                    W(gp, "gp", mm, p2_mm(4 * (ob - 1)))
                src = convw[:, :, ob * 128:(ob + 1) * 128].rearrange(
                    "k (ib p) o -> p k ib o", p=128)
                q = qw_[ob % 2]
                gp.dma_start(out=p2_cw[ob % 2][:].rearrange(
                    "p (k ib o) -> p k ib o", k=4, ib=16), in_=src).then_inc(q.h, 16)
                q.n += 16

        @block.tensor
        def _(te):
            gate(te, "te")
            for ob in range(HB):
                W(te, "te", qw_[ob % 2], qw_p2[ob % 2] + 16 * (ob // 2 + 1))
                for t5 in range(NT5):
                    di = 4 * ob + t5
                    if di >= 4:
                        W(te, "te", ea, p2_ea(di - 4))
                    bank = psC2[di % 4]
                    first = True
                    for k in (3, 2, 1, 0):
                        shift = k - 3
                        for ib in range(HB):
                            lhs = p2_cw[ob % 2][:, (k * 16 + ib) * 128:
                                                (k * 16 + ib) * 128 + 128]
                            t0 = t5 * 512 + shift
                            if t0 >= 0:
                                rhs = AR2[:, ib * 2048 + t0: ib * 2048 + t0 + 512]
                                outp = bank[:]
                            else:
                                cut = -t0
                                rhs = AR2[:, ib * 2048: ib * 2048 + 512 - cut]
                                outp = bank[:, cut:512]
                            inst = te.matmul(outp, lhs, rhs, start=first,
                                             stop=(k == 0 and ib == HB - 1))
                            first = False
                    mm.bump(inst)
                    assert mm.n == p2_mm(di)

        @block.scalar
        def _(sc):
            gate(sc, "sc")
            for di in range(4 * HB):
                ob, t5 = divmod(di, NT5)
                W(sc, "sc", mm, p2_mm(di))
                ea.bump(sc.activation(
                    out=AR1[:, ob * 2048 + t5 * 512: ob * 2048 + t5 * 512 + 512],
                    in_=psC2[di % 4][:], func=AF.Silu, bias=cb_c[:, ob:ob + 1]))
                assert ea.n == p2_ea(di)

        if "xc" in taps:
            @block.sync
            def _(sy):
                W(sy, "sy", ea, p2_ea(4 * HB - 1))
                sy.dma_start(out=tap_t["xc"], in_=AR1[:]).then_inc(qs[3].h, 16)
                qs[3].n += 16
        if stage < 3:
            _finish(nc, block, W, gate, qs, ea, y_out, p7_o, AR1, stbf, take_snap)
            return nc

        take_snap()
        ppsum.close()
        ppsum = ExitStack()
        psTR = [ppsum.enter_context(nc.psum_tensor("psTR3%d" % i, [128, 128], BF16))
                for i in range(2)]
        mm_pp = mm.n
        ea_pp = ea.n
        qw_pp = [qw_[0].n, qw_[1].n]

        @block.gpsimd
        def _(gp):
            gate(gp, "gp")
            for ob in range(HB):
                if ob >= 2:
                    W(gp, "gp", mm, mm_pp + 16 * (ob - 1))
                q = qw_[ob % 2]
                gp.dma_start(out=p3_oprow[ob % 2][:],
                             in_=opw[ob * 128:(ob + 1) * 128, :]).then_inc(q.h, 16)
                q.n += 16
            gp.dma_start(out=p3_xpr[:], in_=xpw).then_inc(qd[0].h, 16)
            qd[0].n += 16
            gp.dma_start(out=p3_dtr[:].rearrange("p (hb r) -> p hb r", r=64),
                         in_=dtw.rearrange("(hb p) r -> p hb r", p=128)
                         ).then_inc(qd[1].h, 16)
            qd[1].n += 16

        @block.tensor
        def _(te):
            gate(te, "te")
            for ob in range(HB):
                W(te, "te", qw_[ob % 2], qw_pp[ob % 2] + 16 * (ob // 2 + 1))
                for hb in range(HB):
                    ti = ob * HB + hb
                    if ti >= 2:
                        W(te, "te", ea, ea_pp + ti - 1)
                    mm.bump(te.transpose(psTR[ti % 2][:],
                                         p3_oprow[ob % 2][:, hb * 128:(hb + 1) * 128],
                                         ident[:]))
            W(te, "te", qd[0], qd[0].n)
            for hb in range(HB):
                ti = 256 + hb
                W(te, "te", ea, ea_pp + ti - 1)
                mm.bump(te.transpose(psTR[ti % 2][:, 0:96],
                                     p3_xpr[:, hb * 128:(hb + 1) * 128],
                                     ident[0:96, 0:96]))
            W(te, "te", qd[1], qd[1].n)
            for hb in range(HB):
                ti = 272 + hb
                W(te, "te", ea, ea_pp + ti - 1)
                mm.bump(te.transpose(psTR[ti % 2][0:64, :],
                                     p3_dtr[:, hb * 64:(hb + 1) * 64], ident[:]))

        @block.scalar
        def _(sc):
            gate(sc, "sc")
            for ob in range(HB):
                for hb in range(HB):
                    ti = ob * HB + hb
                    W(sc, "sc", mm, mm_pp + ti + 1)
                    ea.bump(sc.copy(
                        out=AR2[:, hb * 2048 + ob * 128: hb * 2048 + ob * 128 + 128],
                        in_=psTR[ti % 2][:]))
            for hb in range(HB):
                ti = 256 + hb
                W(sc, "sc", mm, mm_pp + ti + 1)
                ea.bump(sc.copy(out=xpwT[:, hb * 96:(hb + 1) * 96],
                                in_=psTR[ti % 2][:, 0:96]))
            for hb in range(HB):
                ti = 272 + hb
                W(sc, "sc", mm, mm_pp + ti + 1)
                ea.bump(sc.copy(out=dtwT[:, hb * 128:(hb + 1) * 128],
                                in_=psTR[ti % 2][0:64, :]))

        @block.vector
        def _(v):
            ev.bump(v.memset(stat[:], 0.0))

        take_snap()
        ppsum.close()
        ppsum = ExitStack()
        pe_ = ppsum.enter_context
        psA = pe_(nc.psum_tensor("psA", [64, 256], F32))
        psB = pe_(nc.psum_tensor("psB", [16, 256], F32))
        psC = pe_(nc.psum_tensor("psC", [16, 256], F32))
        psD3 = [pe_(nc.psum_tensor("psD3%d" % i, [128, 256], F32)) for i in range(2)]
        psBC = [pe_(nc.psum_tensor("psBC%d" % i, [128, 256], F32)) for i in range(2)]
        psO = pe_(nc.psum_tensor("psO", [128, 512], F32))

        mm3 = mm.n
        ea3 = ea.n
        ev3 = ev.n
        qa3 = qa[0].n
        qs3 = [qs[0].n, qs[1].n]
        MM_CH = 59
        EA_CH = 315
        EV_CH = 288

        def v_mm(c, off):
            return mm3 + MM_CH * c + off

        def v_ea(c, off):
            return ea3 + EA_CH * c + off

        def v_ev(c, off):
            return ev3 + EV_CH * c + off

        def v_dfin(c, hb):
            return v_ev(c, 18 * hb + 1)

        def v_scan(c, gl):
            hb, n = divmod(gl, NST)
            return v_ev(c, 18 * hb + 2 + n)

        def v_qs3(c, d):
            return qs3[d % 2] + 16 * (4 * c + d // 2 + 1)

        zrdy = {}

        @block.sync
        def _(sy):
            gate(sy, "sy")
            src = z_sp[:, 0:TC].rearrange("(hb p) t -> p hb t", p=128)
            sy.dma_start(out=p3_z[:].rearrange("p (hb t) -> p hb t", t=TC),
                         in_=src).then_inc(qa[0].h, 16)
            qa[0].n += 16
            zrdy[0] = qa[0].n

        for c in range(NCH):
            cT = c * TC

            @block.tensor
            def _(te, c=c):
                if c == 0:
                    gate(te, "te")
                if c > 0:
                    W(te, "te", ea, v_ea(c - 1, 3))
                for part, ps, lo, hi in ((1, psA, 0, 64), (2, psB, 64, 80),
                                         (3, psC, 80, 96)):
                    for hb in range(HB):
                        inst = te.matmul(
                            ps[:], xpwT[:, hb * 96 + lo: hb * 96 + hi],
                            AR1[:, hb * 2048 + cT: hb * 2048 + cT + TC],
                            start=(hb == 0), stop=(hb == HB - 1))
                    mm.bump(inst)
                    assert mm.n == v_mm(c, part)
                W(te, "te", ea, v_ea(c, 3))
                for n in range(NST):
                    for half in (0, 1):
                        if n >= 1:
                            W(te, "te", ea, v_ea(c, 4 + 2 * (n - 1) + half))
                        elif c > 0:
                            W(te, "te", ea, v_ea(c - 1, 4 + 30 + half))
                        mm.bump(te.matmul(psBC[half][:],
                                          sel[:, n * 128:(n + 1) * 128],
                                          bmcm[:, half * 256:half * 256 + TC],
                                          start=True, stop=True))
                        assert mm.n == v_mm(c, 4 + 2 * n + half)
                W(te, "te", ea, v_ea(c, 2))
                for hb in range(HB):
                    if hb >= 2:
                        W(te, "te", ea, v_ea(c, 36 + 17 * (hb - 2)))
                    elif c > 0:
                        W(te, "te", ea, v_ea(c - 1, 36 + 17 * (14 + hb)))
                    mm.bump(te.matmul(psD3[hb % 2][:],
                                      dtwT[:, hb * 128:(hb + 1) * 128],
                                      din[:], start=True, stop=True))
                    assert mm.n == v_mm(c, 36 + hb)

            @block.scalar
            def _(sc, c=c):
                if c == 0:
                    gate(sc, "sc")
                W(sc, "sc", qa[0], zrdy[c])
                if c > 0:
                    W(sc, "sc", ev, v_ev(c - 1, EV_CH))
                for hb in range(HB):
                    inst = sc.activation(out=p3_sz[:, hb * 256:(hb + 1) * 256],
                                         in_=p3_z[:, hb * 256:(hb + 1) * 256],
                                         func=AF.Silu)
                ea.bump(inst)
                assert ea.n == v_ea(c, 1)
                W(sc, "sc", mm, v_mm(c, 1))
                ea.bump(sc.copy(out=din[:], in_=psA[:]))
                W(sc, "sc", mm, v_mm(c, 3))
                sc.copy(out=bmcm[:, 0:TC], in_=psB[:])
                ea.bump(sc.copy(out=bmcm[:, 256:256 + TC], in_=psC[:]))
                assert ea.n == v_ea(c, 3)
                for n in range(NST):
                    for half in (0, 1):
                        W(sc, "sc", mm, v_mm(c, 4 + 2 * n + half))
                        ea.bump(sc.copy(
                            out=p3_bc[:, (2 * n + half) * 256:
                                      (2 * n + half) * 256 + TC],
                            in_=psBC[half][:]))
                        assert ea.n == v_ea(c, 4 + 2 * n + half)
                for hb in range(HB):
                    W(sc, "sc", mm, v_mm(c, 36 + hb))
                    gl = c * HB + hb
                    if gl >= 2:
                        pc, ph = divmod(gl - 2, HB)
                        W(sc, "sc", ev, v_dfin(pc, ph))
                    sc.activation(out=p3_xs[hb % 2][:], in_=psD3[hb % 2][:],
                                  func=AF.Identity, bias=dtb_c[:, hb:hb + 1])
                    v_ = sea.bump(sc.activation(out=p3_tl[hb % 2][:],
                                                in_=psD3[hb % 2][:],
                                                func=AF.Abs,
                                                bias=dtb_c[:, hb:hb + 1]))
                    W(sc, "sc", sea, v_)
                    v_ = sea.bump(sc.activation(out=p3_tl[hb % 2][:],
                                                in_=p3_tl[hb % 2][:],
                                                func=AF.Exp, scale=-1.0))
                    W(sc, "sc", sea, v_)
                    ea.bump(sc.activation(out=p3_tl[hb % 2][:],
                                          in_=p3_tl[hb % 2][:],
                                          func=AF.Ln, bias=1.0))
                    assert ea.n == v_ea(c, 36 + 17 * hb)
                    W(sc, "sc", ev, v_dfin(c, hb))
                    for n in range(NST):
                        gl2 = hb * NST + n
                        if gl2 >= 3:
                            W(sc, "sc", ev, v_scan(c, gl2 - 3))
                        ea.bump(sc.activation(
                            out=p3_a[gl2 % 3][:], in_=p3_dl[hb % 4][:],
                            func=AF.Exp,
                            scale=a_c[:, hb * 16 + n: hb * 16 + n + 1]))
                        assert ea.n == v_ea(c, 37 + 17 * hb + n)

            @block.vector
            def _(v, c=c):
                if c == 0:
                    gate(v, "v")
                if c > 0:
                    W(v, "v", ev, v_ev(c - 1, EV_CH))
                if c > 0:
                    W(v, "v", mm, v_mm(c - 1, 59))
                for hb in range(HB):
                    W(v, "v", ea, v_ea(c, 36 + 17 * hb))
                    gl = c * HB + hb
                    if gl >= 4:
                        pc, ph = divmod(gl - 4, HB)
                        W(v, "v", ea, v_ea(pc, 37 + 17 * ph + NST - 1))
                    ev.bump(v.scalar_tensor_tensor(p3_dl[hb % 4][:],
                                                   p3_xs[hb % 2][:], 0.0,
                                                   p3_tl[hb % 2][:],
                                                   op0=ALU.max, op1=ALU.add))
                    assert ev.n == v_dfin(c, hb)
                    v.tensor_mul(p3_du[:], p3_dl[hb % 4][:],
                                 AR1[:, hb * 2048 + cT: hb * 2048 + cT + TC])
                    for n in range(NST):
                        gl2 = hb * NST + n
                        W(v, "v", ea, v_ea(c, 37 + 17 * hb + n))
                        W(v, "v", ea, v_ea(c, 4 + 2 * n + 1))
                        hs_sl = p3_hs4[:, (n % 4) * 256:(n % 4) * 256 + TC]
                        v.tensor_mul(p3_y[:], p3_du[:],
                                     p3_bc[:, (2 * n) * 256:(2 * n) * 256 + TC])
                        init = 0.0 if c == 0 else \
                            stat[:, hb * 16 + n: hb * 16 + n + 1]
                        ev.bump(v.tensor_tensor_scan(hs_sl, p3_a[gl2 % 3][:],
                                                     p3_y[:], init,
                                                     op0=ALU.mult, op1=ALU.add))
                        assert ev.n == v_scan(c, gl2)
                        if n % 4 == 3:
                            v.tensor_copy(
                                stat[:, hb * 16 + n - 3: hb * 16 + n + 1]
                                .rearrange("p (n o) -> p n o", o=1),
                                p3_hs4[:].rearrange("p (n s) -> p n s", s=256)
                                [:, :, TC - 1:TC])
                            hsv = p3_hs4[:].rearrange("p (n s) -> p n s", s=256)
                            c4 = p3_bc[:].rearrange("p (n s) -> p n s", s=512)[
                                :, n - 3:n + 1, 256:512]
                            v.tensor_mul(hsv, hsv, c4)
                            v.tensor_add(p3_t1[:], p3_hs4[:, 0:512],
                                         p3_hs4[:, 512:1024])
                            if n == 3:
                                v.tensor_add(p3_ya[:], p3_t1[:, 0:256],
                                             p3_t1[:, 256:512])
                            else:
                                v.tensor_add(p3_y2[:], p3_t1[:, 0:256],
                                             p3_t1[:, 256:512])
                                v.tensor_add(p3_ya[:], p3_ya[:], p3_y2[:])
                    v.scalar_tensor_tensor(p3_y[:],
                                           AR1[:, hb * 2048 + cT:
                                               hb * 2048 + cT + TC],
                                           d_c[:, hb:hb + 1],
                                           p3_ya[:], op0=ALU.mult, op1=ALU.add)
                    W(v, "v", ea, v_ea(c, 1))
                    ev.bump(v.tensor_mul(p3_yg[:, hb * 256:(hb + 1) * 256],
                                         p3_y[:],
                                         p3_sz[:, hb * 256:(hb + 1) * 256]))
                    assert ev.n == v_ev(c, 18 * hb + 18)

            @block.tensor
            def _(te, c=c):
                W(te, "te", ev, v_ev(c, EV_CH))
                for d in range(8):
                    toff, h5 = divmod(d, 4)
                    if d > 0:
                        W(te, "te", ea, v_ea(c, 308 + d - 1))
                    elif c > 0:
                        W(te, "te", ea, v_ea(c - 1, 315))
                    for ob in range(HB):
                        inst = te.matmul(
                            psO[:],
                            p3_yg[:, ob * 256 + toff * 128:
                                  ob * 256 + toff * 128 + 128],
                            AR2[:, ob * 2048 + h5 * 512: ob * 2048 + h5 * 512 + 512],
                            start=(ob == 0), stop=(ob == HB - 1))
                    mm.bump(inst)
                    assert mm.n == v_mm(c, 52 + d)

            @block.scalar
            def _(sc, c=c):
                for d in range(8):
                    W(sc, "sc", mm, v_mm(c, 52 + d))
                    gd = c * 8 + d
                    if gd >= 2:
                        pc, pd = divmod(gd - 2, 8)
                        W(sc, "sc", qs[pd % 2], v_qs3(pc, pd))
                    slot = d % 2
                    ea.bump(sc.copy(out=stbf[:, slot * 512:(slot + 1) * 512],
                                    in_=psO[:]))
                    assert ea.n == v_ea(c, 308 + d)

            @block.sync
            def _(sy, c=c):
                for d in range(8):
                    toff, h5 = divmod(d, 4)
                    W(sy, "sy", ea, v_ea(c, 308 + d))
                    slot = d % 2
                    q = qs[slot]
                    sy.dma_start(
                        out=mo_sp[cT + toff * 128: cT + toff * 128 + 128,
                                  h5 * 512:(h5 + 1) * 512],
                        in_=stbf[:, slot * 512:(slot + 1) * 512]).then_inc(q.h, 16)
                    q.n += 16
                    assert q.n == v_qs3(c, d)
                if c + 1 < NCH:
                    W(sy, "sy", ea, v_ea(c, 1))
                    src = z_sp[:, (c + 1) * TC:(c + 2) * TC].rearrange(
                        "(hb p) t -> p hb t", p=128)
                    sy.dma_start(out=p3_z[:].rearrange("p (hb t) -> p hb t", t=TC),
                                 in_=src).then_inc(qa[0].h, 16)
                    qa[0].n += 16
                    zrdy[c + 1] = qa[0].n

        if stage < 4:
            _finish(nc, block, W, gate, qs, ea, y_out, p7_o, AR1, stbf, take_snap)
            return nc

        take_snap()
        ppsum.close()
        ppsum = ExitStack()
        psTR = [ppsum.enter_context(nc.psum_tensor("psTR4%d" % i, [128, 128], BF16))
                for i in range(2)]
        psL = ppsum.enter_context(nc.psum_tensor("psL", [128, 32], F32))
        psS = ppsum.enter_context(nc.psum_tensor("psS", [128, 16], F32))
        ea4 = ea.n
        ev4 = ev.n
        EA4 = 18
        EV4 = 3

        def p4_ea(tb, off):
            return ea4 + EA4 * tb + off

        def p4_ev(tb, off):
            return ev4 + EV4 * tb + off

        p4_mm = {}

        p4_ld = []

        @block.sync
        def _(sy):
            gate(sy, "sy")
            for tb in range(TB):
                if tb >= 2:
                    W(sy, "sy", ev, p4_ev(tb - 2, 1))
                    W(sy, "sy", ea, p4_ea(tb - 2, 1))
                sy.dma_start(out=p4_x[tb % 2][:],
                             in_=x_in[tb * 128:(tb + 1) * 128, :]
                             ).then_inc(qa[tb % 2].h, 16)
                qa[tb % 2].n += 16
                sy.dma_start(out=p4_mo[tb % 2][:],
                             in_=mo_sp[tb * 128:(tb + 1) * 128, :]
                             ).then_inc(qb[tb % 2].h, 16)
                qb[tb % 2].n += 16
                p4_ld.append((qa[tb % 2].n, qb[tb % 2].n))

        for tb in range(TB):
            @block.scalar
            def _(sc, tb=tb):
                if tb == 0:
                    gate(sc, "sc")
                W(sc, "sc", qa[tb % 2], p4_ld[tb][0])
                if tb >= 1:
                    W(sc, "sc", ev, p4_ev(tb - 1, 1))
                    W(sc, "sc", mm, p4_mm.get((tb - 1, HB - 1), 0))
                v_ = sea.bump(sc.activation(out=p4_xn[:], in_=p4_x[tb % 2][:],
                                            func=AF.Identity,
                                            scale=rstd_c[:, tb:tb + 1],
                                            bias=nmr_c[:, tb:tb + 1]))
                W(sc, "sc", sea, v_)
                ea.bump(sc.copy(out=stbf[:], in_=p4_xn[:]))
                assert ea.n == p4_ea(tb, 1)

            @block.vector
            def _(v, tb=tb):
                if tb == 0:
                    gate(v, "v")
                W(v, "v", ea, p4_ea(tb, 1))
                W(v, "v", qb[tb % 2], p4_ld[tb][1])
                v.tensor_add(p4_t1[:], p4_x[tb % 2][:], p4_xn[:])
                ev.bump(v.tensor_add(AR2[:, tb * 2048:(tb + 1) * 2048],
                                     p4_t1[:], p4_mo[tb % 2][:]))
                assert ev.n == p4_ev(tb, 1)

            @block.tensor
            def _(te, tb=tb):
                if tb == 0:
                    gate(te, "te")
                W(te, "te", ea, p4_ea(tb, 1))
                for hb in range(HB):
                    if tb > 0 or hb >= 2:
                        ptb, phb = (tb, hb - 2) if hb >= 2 else (tb - 1, HB - 2 + hb)
                        W(te, "te", ea, p4_ea(ptb, 2 + phb))
                    mm.bump(te.transpose(psTR[hb % 2][:],
                                         stbf[:, hb * 128:(hb + 1) * 128],
                                         ident[:]))
                    p4_mm[(tb, hb)] = mm.n

            @block.scalar
            def _(sc, tb=tb):
                for hb in range(HB):
                    W(sc, "sc", mm, p4_mm[(tb, hb)])
                    ea.bump(sc.copy(
                        out=AR1[:, hb * 2048 + tb * 128: hb * 2048 + tb * 128 + 128],
                        in_=psTR[hb % 2][:]))
                    assert ea.n == p4_ea(tb, 2 + hb)

            @block.tensor
            def _(te, tb=tb):
                W(te, "te", ea, p4_ea(tb, 17))
                if tb >= 1:
                    W(te, "te", ea, p4_ea(tb - 1, 18))
                for hb in range(HB):
                    inst = te.matmul(psL[:],
                                     AR1[:, hb * 2048 + tb * 128:
                                         hb * 2048 + tb * 128 + 128],
                                     gateT[:, hb * 32:(hb + 1) * 32],
                                     start=(hb == 0), stop=(hb == HB - 1))
                mm.bump(inst)
                p4_mm[("lg", tb)] = mm.n
                for hb in range(HB):
                    inst = te.matmul(psS[:, tb:tb + 1],
                                     AR1[:, hb * 2048 + tb * 128:
                                         hb * 2048 + tb * 128 + 128],
                                     segT[:, hb:hb + 1],
                                     start=(hb == 0), stop=(hb == HB - 1))
                mm.bump(inst)
                p4_mm[("sg", tb)] = mm.n

            @block.vector
            def _(v, tb=tb):
                W(v, "v", mm, p4_mm[("lg", tb)])
                ev.bump(v.reduce_max(smt[:, 16:17], psL[:], axis=AX.X, negate=True))
                assert ev.n == p4_ev(tb, 2)

            @block.scalar
            def _(sc, tb=tb):
                W(sc, "sc", ev, p4_ev(tb, 2))
                ea.bump(sc.activation(out=smt[:, 32:64], in_=psL[:], func=AF.Exp,
                                      bias=smt[:, 16:17]))
                assert ea.n == p4_ea(tb, 18)

            @block.vector
            def _(v, tb=tb):
                W(v, "v", ea, p4_ea(tb, 18))
                ops = [
                    lambda: v.reduce_sum(smt[:, 17:18], smt[:, 32:64], axis=AX.X),
                    lambda: v.reciprocal(smt[:, 18:19], smt[:, 17:18]),
                    lambda: v.tensor_scalar_mul(smt[:, 32:64], smt[:, 32:64],
                                                smt[:, 18:19]),
                    lambda: v.max(out=mx8[:], in_=smt[:, 32:64]),
                    lambda: v.tensor_single_scalar(comb[:, tb * 32:(tb + 1) * 32],
                                                   smt[:, 32:64], mx8[:, 3:4],
                                                   op=ALU.is_ge),
                    lambda: v.tensor_mul(comb[:, tb * 32:(tb + 1) * 32],
                                         comb[:, tb * 32:(tb + 1) * 32],
                                         smt[:, 32:64]),
                    lambda: v.reduce_sum(smt[:, 19:20],
                                         comb[:, tb * 32:(tb + 1) * 32],
                                         axis=AX.X),
                    lambda: v.reciprocal(smt[:, 20:21], smt[:, 19:20]),
                ]
                for op in ops:
                    v_ = sev.bump(op())
                    W(v, "v", sev, v_)
                ev.bump(v.tensor_scalar_mul(comb[:, tb * 32:(tb + 1) * 32],
                                            comb[:, tb * 32:(tb + 1) * 32],
                                            smt[:, 20:21]))
                assert ev.n == p4_ev(tb, 3)

        @block.scalar
        def _(sc):
            W(sc, "sc", mm, p4_mm[("sg", TB - 1)])
            ea.bump(sc.activation(out=s_c[:], in_=psS[:], func=AF.Sigmoid))

        if "comb" in taps:
            @block.vector
            def _(v):
                ev.bump(v.tensor_copy(AR1[:, 0:512], comb[:]))

            @block.sync
            def _(sy):
                W(sy, "sy", ev)
                sy.dma_start(out=tap_t["comb"][:, 0:512],
                             in_=AR1[:, 0:512]).then_inc(qs[3].h, 16)
                qs[3].n += 16
        if stage < 5:
            _finish(nc, block, W, gate, qs, ea, y_out, p7_o, AR2, stbf, take_snap)
            return nc

        take_snap()
        ppsum.close()
        ppsum = ExitStack()
        psG = [ppsum.enter_context(nc.psum_tensor("psG%d" % i, [128, 512], F32))
               for i in range(2)]
        psU = [ppsum.enter_context(nc.psum_tensor("psU%d" % i, [128, 512], F32))
               for i in range(2)]
        psD5 = [ppsum.enter_context(nc.psum_tensor("psD5%d" % i, [128, 512], F32))
                for i in range(4)]

        st5 = {"mm_wl": {}, "mm_dl": {}}

        def expert_phase(wgw, wuw, wdw, nfb, wgt_c, wut_c, wdt_b, hbuf, combcol):
            mmE = mm.n
            eaE = ea.n
            evE = ev.n
            qgE = [qg[0].n, qg[1].n]
            quE = [qu[0].n, qu[1].n]
            qdE = [qd[0].n, qd[1].n]
            nup = nfb * NT5

            def m_g(ui):
                return mmE + 2 * ui + 1

            def m_u(ui):
                return mmE + 2 * ui + 2

            def m_d(di):
                return mmE + 2 * nup + di + 1

            def a_sg(ui):
                return eaE + ui + 1

            def v_he(ui):
                return evE + ui + 1

            def v_de(di):
                return evE + nup + di + 1

            @block.gpsimd
            def _(gp):
                for f in range(nfb):
                    pv = st5["mm_wl"].get(f % 2, 0)
                    if pv:
                        W(gp, "gp", mm, pv)
                    src = wgw.rearrange("(hb p) (fb f) -> fb hb p f",
                                        p=128, f=128)[f].transpose([1, 0, 2])
                    gp.dma_start(out=wgt_c[f % 2][:].rearrange(
                        "p (hb f) -> p hb f", f=128), in_=src
                        ).then_inc(qg[f % 2].h, 16)
                    qg[f % 2].n += 16
                    src = wuw.rearrange("(hb p) (fb f) -> fb hb p f",
                                        p=128, f=128)[f].transpose([1, 0, 2])
                    gp.dma_start(out=wut_c[f % 2][:].rearrange(
                        "p (hb f) -> p hb f", f=128), in_=src
                        ).then_inc(qu[f % 2].h, 16)
                    qu[f % 2].n += 16
                    st5["mm_wl"][f % 2] = m_u((f + 1) * NT5 - 1)
                for h5 in range(4):
                    pv = st5["mm_dl"].get(h5 % 2, 0)
                    if pv:
                        W(gp, "gp", mm, pv)
                    src = wdw[:, h5 * 512:(h5 + 1) * 512].rearrange(
                        "(fb p) h -> p fb h", p=128)
                    gp.dma_start(out=wdt_b[h5 % 2][:, 0:nfb * 512].rearrange(
                        "p (fb h) -> p fb h", h=512), in_=src
                        ).then_inc(qd[h5 % 2].h, 16)
                    qd[h5 % 2].n += 16
                    st5["mm_dl"][h5 % 2] = m_d((h5 + 1) * TB - 1)

            @block.tensor
            def _(te):
                for f in range(nfb):
                    W(te, "te", qg[f % 2], qgE[f % 2] + 16 * (f // 2 + 1))
                    W(te, "te", qu[f % 2], quE[f % 2] + 16 * (f // 2 + 1))
                    for t5 in range(NT5):
                        ui = f * NT5 + t5
                        if ui >= 2:
                            W(te, "te", ea, a_sg(ui - 2))
                            W(te, "te", ev, v_he(ui - 2))
                        for hb in range(HB):
                            inst = te.matmul(
                                psG[ui % 2][:],
                                wgt_c[f % 2][:, hb * 128:(hb + 1) * 128],
                                AR1[:, hb * 2048 + t5 * 512:
                                    hb * 2048 + t5 * 512 + 512],
                                start=(hb == 0), stop=(hb == HB - 1))
                        mm.bump(inst)
                        assert mm.n == m_g(ui)
                        for hb in range(HB):
                            inst = te.matmul(
                                psU[ui % 2][:],
                                wut_c[f % 2][:, hb * 128:(hb + 1) * 128],
                                AR1[:, hb * 2048 + t5 * 512:
                                    hb * 2048 + t5 * 512 + 512],
                                start=(hb == 0), stop=(hb == HB - 1))
                        mm.bump(inst)
                        assert mm.n == m_u(ui)
                W(te, "te", ev, v_he(nup - 1))
                for h5 in range(4):
                    W(te, "te", qd[h5 % 2], qdE[h5 % 2] + 16 * (h5 // 2 + 1))
                    for tb in range(TB):
                        di = h5 * TB + tb
                        if di >= 4:
                            W(te, "te", ev, v_de(di - 4))
                        for fbk in range(nfb):
                            inst = te.matmul(
                                psD5[di % 4][:],
                                hbuf[:, fbk * 2048 + tb * 128:
                                     fbk * 2048 + tb * 128 + 128],
                                wdt_b[h5 % 2][:, fbk * 512:(fbk + 1) * 512],
                                start=(fbk == 0), stop=(fbk == nfb - 1))
                        mm.bump(inst)
                        assert mm.n == m_d(di)

            @block.scalar
            def _(sc):
                for ui in range(nup):
                    W(sc, "sc", mm, m_g(ui))
                    if ui >= 2:
                        W(sc, "sc", ev, v_he(ui - 2))
                    ea.bump(sc.activation(out=p5_sg[ui % 2][:], in_=psG[ui % 2][:],
                                          func=AF.Silu))
                    assert ea.n == a_sg(ui)

            @block.vector
            def _(v):
                W(v, "v", ev, evE)
                for ui in range(nup):
                    f, t5 = divmod(ui, NT5)
                    W(v, "v", ea, a_sg(ui))
                    W(v, "v", mm, m_u(ui))
                    ev.bump(v.tensor_mul(
                        hbuf[:, f * 2048 + t5 * 512: f * 2048 + t5 * 512 + 512],
                        p5_sg[ui % 2][:], psU[ui % 2][:]))
                    assert ev.n == v_he(ui)
                for di in range(4 * TB):
                    h5, tb = divmod(di, TB)
                    W(v, "v", mm, m_d(di))
                    accs = AR2[:, tb * 2048 + h5 * 512: tb * 2048 + h5 * 512 + 512]
                    ev.bump(v.scalar_tensor_tensor(accs, psD5[di % 4][:],
                                                   combcol(tb), accs,
                                                   op0=ALU.mult, op1=ALU.add))
                    assert ev.n == v_de(di)

        @block.gpsimd
        def _(gp):
            gate(gp, "gp")

        @block.tensor
        def _(te):
            gate(te, "te")

        for ee in range(NEXP):
            expert_phase(wg_w[ee], wu_w[ee], wd_w[ee], 6,
                         p5_wg, p5_wu, p5_wd, p5_he,
                         lambda tb, ee=ee: comb[:, tb * 32 + ee: tb * 32 + ee + 1])

        if stage >= 6:
            take_snap()
            @block.gpsimd
            def _(gp):
                gate(gp, "gp")

            @block.tensor
            def _(te):
                gate(te, "te")
            st5["mm_wl"] = {}
            st5["mm_dl"] = {}
            for fc in range(SF // 512):
                expert_phase(sg_w[:, fc * 512:(fc + 1) * 512],
                             su_w[:, fc * 512:(fc + 1) * 512],
                             sd_w[fc * 512:(fc + 1) * 512, :], 4,
                             p6_wg, p6_wu, p6_wd, p6_hs,
                             lambda tb: s_c[:, tb:tb + 1])

        _finish(nc, block, W, gate, qs, ea, y_out, p7_o, AR2, stbf, take_snap)
    return nc


def _finish(nc, block, W, gate, qs, ea, y_out, p7_o, accbuf, stbf, take_snap):
    take_snap()
    fin = {}
    qsF = [qs[0].n, qs[1].n]

    @block.scalar
    def _(sc):
        gate(sc, "sc")
        for tb in range(TB):
            if tb >= 2:
                W(sc, "sc", qs[tb % 2], qsF[tb % 2] + 16 * ((tb - 2) // 2 + 1))
            ea.bump(sc.copy(out=p7_o[tb % 2][:],
                            in_=accbuf[:, tb * 2048:(tb + 1) * 2048]))
            fin[("e", tb)] = ea.n

    @block.sync
    def _(sy):
        gate(sy, "sy")
        for tb in range(TB):
            W(sy, "sy", ea, fin[("e", tb)])
            q = qs[tb % 2]
            sy.dma_start(out=y_out[tb * 128:(tb + 1) * 128, :],
                         in_=p7_o[tb % 2][:]).then_inc(q.h, 16)
            q.n += 16
        sy.wait_ge(qs[0].h, qs[0].n)
        sy.wait_ge(qs[1].h, qs[1].n)


def kernel(**inputs):
    x = np.ascontiguousarray(np.asarray(inputs["L0_attn_x"], dtype=np.float32))
    shared = {
        "norm_w": inputs["L0_attn_norm_w"],
        "in_proj_w": inputs["L0_attn_in_proj_w"],
        "conv_w": inputs["L0_attn_conv_w"],
        "conv_b": inputs["L0_attn_conv_b"],
        "x_proj_w": inputs["L0_attn_x_proj_w"],
        "dt_proj_w": inputs["L0_attn_dt_proj_w"],
        "dt_proj_b": inputs["L0_attn_dt_proj_b"],
        "A_log": inputs["L0_attn_A_log"],
        "D": inputs["L0_attn_D"],
        "out_proj_w": inputs["L0_attn_out_proj_w"],
        "gate_w": inputs["L0_ffn_gate_w"],
        "wg": inputs["L0_ffn_wg"],
        "wu": inputs["L0_ffn_wu"],
        "wd": inputs["L0_ffn_wd"],
        "sg": inputs["L0_ffn_sg"],
        "su": inputs["L0_ffn_su"],
        "sd": inputs["L0_ffn_sd"],
        "seg": inputs["L0_ffn_seg"],
    }
    shared = {k: np.ascontiguousarray(np.asarray(v, dtype=np.float32))
              for k, v in shared.items()}
    nc = build()
    in_maps = [dict(x=x[c], **shared) for c in range(8)]
    res = run_bass_kernel_spmd(nc, in_maps, core_ids=list(range(8)))
    out = np.stack([np.asarray(res.results[c]["y"]) for c in range(8)], axis=0)
    return out.astype(np.float32)


# version 10
# speedup vs baseline: 1.0918x; 1.0056x over previous
import numpy as np
import ml_dtypes
from contextlib import ExitStack

import concourse.bass as bass
import concourse.mybir as mybir
from concourse.bass_utils import run_bass_kernel_spmd

F32 = mybir.dt.float32
BF16 = mybir.dt.bfloat16
AF = mybir.ActivationFunctionType
ALU = mybir.AluOpType
AX = mybir.AxisListType

L = 2048
H = 2048
HB = 16
TB = 16
TC = 256
NCH = L // TC
DTR = 64
NST = 16
FEXP = 768
NEXP = 32
SF = 5632
EPS = 1e-5
NT5 = L // 512
DEDUP = True


class Sem:
    def __init__(self, ctx, nc, name):
        self.h = ctx.enter_context(nc.semaphore(name))
        self.n = 0

    def bump(self, inst, k=1):
        inst.then_inc(self.h, k)
        self.n += k
        return self.n


def build(stage=7, taps=()):
    taps = set(taps)
    nc = bass.Bass("TRN2", target_bir_lowering=False)

    def I(name, shape):
        return nc.dram_tensor(name, shape, F32, kind="ExternalInput").ap()

    x_in = I("x", [L, H])
    norm_w = I("norm_w", [H])
    ipw = I("in_proj_w", [2 * H, H])
    convw = I("conv_w", [4, H, H])
    convb = I("conv_b", [H])
    xpw = I("x_proj_w", [96, H])
    dtw = I("dt_proj_w", [H, DTR])
    dtb = I("dt_proj_b", [H])
    alog = I("A_log", [H, NST])
    dvec = I("D", [H])
    opw = I("out_proj_w", [H, H])
    gw = I("gate_w", [H, NEXP])
    wg_w = I("wg", [NEXP, H, FEXP])
    wu_w = I("wu", [NEXP, H, FEXP])
    wd_w = I("wd", [NEXP, FEXP, H])
    sg_w = I("sg", [H, SF])
    su_w = I("su", [H, SF])
    sd_w = I("sd", [SF, H])
    seg_w = I("seg", [H, 1])

    y_out = nc.dram_tensor("y", [L, H], F32, kind="ExternalOutput").ap()
    zk = "ExternalOutput" if "z" in taps else "Internal"
    mk = "ExternalOutput" if "mo" in taps else "Internal"
    z_sp = nc.dram_tensor("z_sp", [H, L], BF16, kind=zk).ap()
    mo_sp = nc.dram_tensor("mo_sp", [L, H], BF16, kind=mk).ap()
    tap_t = {}
    for t in taps - {"z", "mo"}:
        tap_t[t] = nc.dram_tensor("tap_" + t, [128, HB * 2048], BF16,
                                  kind="ExternalOutput").ap()

    ident_d = nc.inline_tensor(np.eye(128, dtype=ml_dtypes.bfloat16), "identc").ap()
    CONSTV = [EPS, 1.0, -1.0, -0.5, 1.0 / H, 0.0, 2.0, 0.5]
    cons_d = nc.inline_tensor(
        np.tile(np.array(CONSTV, np.float32), (128, 1)), "consv").ap()
    sel_np = np.zeros((16, 16 * 128), dtype=ml_dtypes.bfloat16)
    for n in range(16):
        sel_np[n, n * 128:(n + 1) * 128] = 1
    sel_d = nc.inline_tensor(sel_np, "selc").ap()

    with ExitStack() as ctx:
        e = ctx.enter_context
        AR1 = nc.alloc_sbuf_tensor("AR1", [128, HB * 2048], BF16)
        AR2 = nc.alloc_sbuf_tensor("AR2", [128, HB * 2048], BF16)
        ident = nc.alloc_sbuf_tensor("ident", [128, 128], BF16)
        sel = nc.alloc_sbuf_tensor("sel", [16, 2048], BF16)
        mean_c = nc.alloc_sbuf_tensor("mean_c", [128, 16], F32)
        rstd_c = nc.alloc_sbuf_tensor("rstd_c", [128, 16], F32)
        nmr_c = nc.alloc_sbuf_tensor("nmr_c", [128, 16], F32)
        w_c = nc.alloc_sbuf_tensor("w_c", [128, 16], F32)
        cb_c = nc.alloc_sbuf_tensor("cb_c", [128, 16], F32)
        dtb_c = nc.alloc_sbuf_tensor("dtb_c", [128, 16], F32)
        d_c = nc.alloc_sbuf_tensor("d_c", [128, 16], F32)
        s_c = nc.alloc_sbuf_tensor("s_c", [128, 16], F32)
        a_c = nc.alloc_sbuf_tensor("a_c", [128, 256], F32)
        comb = nc.alloc_sbuf_tensor("comb", [128, 512], F32)
        gateT = nc.alloc_sbuf_tensor("gateT", [128, 512], BF16)
        segT = nc.alloc_sbuf_tensor("segT", [128, 16], BF16)
        xpwT = nc.alloc_sbuf_tensor("xpwT", [128, 16 * 96], BF16)
        dtwT = nc.alloc_sbuf_tensor("dtwT", [64, 2048], BF16)
        bmcm = nc.alloc_sbuf_tensor("bmcm", [16, 512], BF16)
        din = nc.alloc_sbuf_tensor("din", [64, 256], BF16)
        stat = nc.alloc_sbuf_tensor("stat", [128, 256], F32)
        stbf = nc.alloc_sbuf_tensor("stbf", [128, 2048], BF16)
        smt = nc.alloc_sbuf_tensor("smt", [128, 64], F32)
        mx8 = nc.alloc_sbuf_tensor("mx8", [128, 8], F32)
        cons = nc.alloc_sbuf_tensor("cons", [128, 8], F32)
        for ci_, cv_ in enumerate(CONSTV):
            nc.const_aps.register(F32, cv_, cons[:, ci_:ci_ + 1])

        FLEX = 57344
        fl = nc.bump_sbuf(FLEX)
        assert fl is not None, "flex arena alloc failed"
        fb = fl[0]

        def flex(name, shape, dtype, off):
            return nc.alloc_sbuf_tensor_at(name, shape, dtype, offset=fb + off)

        p0_x = [flex("p0_x%d" % i, [128, 2048], F32, i * 8192) for i in range(2)]
        p0_xn = flex("p0_xn", [128, 2048], F32, 16384)
        p0_sq = flex("p0_sq", [128, 2048], F32, 24576)
        p0_rtm = flex("p0_rtm", [128, 2048], BF16, 32768)

        p1_row = [flex("p1_row%d" % i, [128, 2048], BF16, i * 4096) for i in range(2)]
        p1_ipt = [flex("p1_ipt%d" % i, [128, 2048], BF16, 8192 + i * 4096)
                  for i in range(2)]

        p2_cw = [flex("p2_cw%d" % i, [128, 8192], BF16, i * 16384) for i in range(2)]

        p3_oprow = [flex("p3_oprow%d" % i, [128, 2048], BF16, i * 4096)
                    for i in range(2)]
        p3_xpr = flex("p3_xpr", [96, 2048], BF16, 8192)
        p3_dtr = flex("p3_dtr", [128, 1024], BF16, 12288)

        p3_bc = flex("p3_bc", [128, 16 * 512], BF16, 0)
        p3_z = flex("p3_z", [128, 4096], BF16, 16384)
        p3_yg = flex("p3_yg", [128, 4096], BF16, 24576)
        p3_sz = flex("p3_sz", [128, 4096], BF16, 32768)
        p3_dl = [flex("p3_dl%d" % i, [128, 256], F32, 40960 + i * 1024)
                 for i in range(4)]
        p3_a = [flex("p3_a%d" % i, [128, 256], F32, 45056 + i * 1024)
                for i in range(3)]
        p3_du = flex("p3_du", [128, 256], F32, 48128)
        p3_xs = [flex("p3_xs%d" % i, [128, 256], F32, 49152 + i * 1024)
                 for i in range(2)]
        p3_tl = [flex("p3_tl%d" % i, [128, 256], F32, 51200 + i * 1024)
                 for i in range(2)]
        p3_hs4 = flex("p3_hs4", [128, 1024], BF16, 53248)
        p3_ya = flex("p3_ya", [128, 256], BF16, 55296)
        p3_y2 = flex("p3_y2", [128, 256], BF16, 55808)
        p3_y = flex("p3_y", [128, 256], F32, 56320)

        p4_x = [flex("p4_x%d" % i, [128, 2048], F32, i * 8192) for i in range(2)]
        p4_mo = [flex("p4_mo%d" % i, [128, 2048], BF16, 16384 + i * 4096)
                 for i in range(2)]
        p4_xn = flex("p4_xn", [128, 2048], F32, 24576)
        p4_t1 = flex("p4_t1", [128, 2048], F32, 32768)

        p5_he = flex("p5_he", [128, 6 * 2048], BF16, 0)
        p5_wg = [flex("p5_wg%d" % i, [128, 2048], BF16, 24576 + i * 4096)
                 for i in range(2)]
        p5_wu = [flex("p5_wu%d" % i, [128, 2048], BF16, 32768 + i * 4096)
                 for i in range(2)]
        p5_wd = [flex("p5_wd%d" % i, [128, 3072], BF16, 40960 + i * 6144)
                 for i in range(2)]
        p5_sg = [flex("p5_sg%d" % i, [128, 512], F32, 53248 + i * 2048)
                 for i in range(2)]

        p6_hs = flex("p6_hs", [128, 4 * 2048], BF16, 0)
        p6_wg = [flex("p6_wg%d" % i, [128, 2048], BF16, 16384 + i * 4096)
                 for i in range(2)]
        p6_wu = [flex("p6_wu%d" % i, [128, 2048], BF16, 24576 + i * 4096)
                 for i in range(2)]
        p6_wd = [flex("p6_wd%d" % i, [128, 2048], BF16, 32768 + i * 4096)
                 for i in range(2)]

        p7_o = [flex("p7_o%d" % i, [128, 2048], F32, i * 8192) for i in range(2)]

        qc = Sem(ctx, nc, "qc")
        qa = [Sem(ctx, nc, "qa%d" % i) for i in range(2)]
        qb = [Sem(ctx, nc, "qb%d" % i) for i in range(2)]
        qs = [Sem(ctx, nc, "qs%d" % i) for i in range(4)]
        qw_ = [Sem(ctx, nc, "qw%d" % i) for i in range(2)]
        qg = [Sem(ctx, nc, "qg%d" % i) for i in range(2)]
        qu = [Sem(ctx, nc, "qu%d" % i) for i in range(2)]
        qd = [Sem(ctx, nc, "qd%d" % i) for i in range(2)]
        mm = Sem(ctx, nc, "mm")
        ea = Sem(ctx, nc, "ea")
        ev = Sem(ctx, nc, "ev")
        sea = Sem(ctx, nc, "sea")
        sev = Sem(ctx, nc, "sev")
        SEMS = [qc] + qa + qb + qs + qw_ + qg + qu + qd + [mm, ea, ev]
        OWN = {"te": mm, "sc": ea, "v": ev}
        last_wait = {}
        snap = {}

        def take_snap():
            for s in SEMS:
                snap[s.h.name] = s.n

        def W(eng, tag, s, v=None):
            v = s.n if v is None else v
            if v is None or v <= 0:
                return
            k = (tag, s.h.name)
            if DEDUP and last_wait.get(k, -1) >= v:
                return
            last_wait[k] = v
            eng.wait_ge(s.h, v)

        def gate(eng, tag):
            for s in SEMS:
                if OWN.get(tag) is s:
                    continue
                W(eng, tag, s, snap[s.h.name])

        block = e(nc.Block())

        @block.sync
        def _(sy):
            sy.dma_start(out=ident[:], in_=ident_d).then_inc(qc.h, 16)
            sy.dma_start(out=cons[:], in_=cons_d).then_inc(qc.h, 16)
            sy.dma_start(out=sel[:], in_=sel_d).then_inc(qc.h, 16)
            for vec, dst in ((norm_w, w_c), (convb, cb_c), (dtb, dtb_c),
                             (dvec, d_c)):
                for hb in range(HB):
                    sy.dma_start(out=dst[:, hb:hb + 1],
                                 in_=vec[hb * 128:(hb + 1) * 128]
                                 ).then_inc(qc.h, 16)
            sy.dma_start(out=a_c[:].rearrange("p (hb n) -> p hb n", n=16),
                         in_=alog.rearrange("(hb p) n -> p hb n", p=128)
                         ).then_inc(qc.h, 16)
            qc.n += 16 * 68

        @block.gpsimd
        def _(gp):
            gp.dma_start(out=gateT[:].rearrange("p (hb e) -> p hb e", e=32),
                         in_=gw.rearrange("(hb p) e -> p hb e", p=128)
                         ).then_inc(qg[0].h, 16)
            qg[0].n += 16
            gp.dma_start(out=segT[:].rearrange("p (hb o) -> p hb o", o=1),
                         in_=seg_w.rearrange("(hb p) o -> p hb o", p=128),
                         allow_slow_non_contiguous=True).then_inc(qg[1].h, 16)
            qg[1].n += 16
        const_qx = qc.n

        @block.scalar
        def _(sc):
            W(sc, "sc", qc, const_qx)
            ea.bump(sc.activation(out=p0_sq[:, 0:256], in_=a_c[:], func=AF.Exp))
        aexp_ea = ea.n

        @block.vector
        def _(v):
            W(v, "v", ea, aexp_ea)
            ev.bump(v.tensor_scalar_mul(a_c[:], p0_sq[:, 0:256], -1.0))
        aneg_ev = ev.n

        if stage < 1:
            _finish(nc, block, W, gate, qs, ea, y_out, p7_o, AR2, stbf, take_snap)
            return nc

        ea_p0 = ea.n
        ev_p0 = ev.n
        ppsum = ExitStack()
        psTR = [ppsum.enter_context(nc.psum_tensor("psTR%d" % i, [128, 128], BF16))
                for i in range(2)]
        EA_TB = 19
        EV_TB = 4

        def p0_ea_v(tb, off):
            return ea_p0 + EA_TB * tb + off

        def p0_ev_v(tb, off):
            return ev_p0 + EV_TB * tb + off

        p0_mm = {}
        p0_ld = []

        @block.sync
        def _(sy):
            for tb in range(TB):
                if tb >= 2:
                    W(sy, "sy", ea, p0_ea_v(tb - 2, 3))
                    W(sy, "sy", ev, p0_ev_v(tb - 2, 1))
                q = qa[tb % 2]
                sy.dma_start(out=p0_x[tb % 2][:],
                             in_=x_in[tb * 128:(tb + 1) * 128, :]).then_inc(q.h, 16)
                q.n += 16
                p0_ld.append((q, q.n))

        for tb in range(TB):
            ld_q, ld_v = p0_ld[tb]

            @block.vector
            def _(v, tb=tb, ld_q=ld_q, ld_v=ld_v):
                W(v, "v", ld_q, ld_v)
                v_ = sev.bump(v.reduce_sum(smt[:, 0:1], p0_x[tb % 2][:],
                                           axis=AX.X))
                W(v, "v", sev, v_)
                sev.bump(v.tensor_scalar_mul(mean_c[:, tb:tb + 1], smt[:, 0:1],
                                             1.0 / H))
                ev.bump(v.tensor_copy(smt[:, 12:13], smt[:, 0:1]))
                assert ev.n == p0_ev_v(tb, 1)

            @block.scalar
            def _(sc, tb=tb, ld_q=ld_q, ld_v=ld_v):
                W(sc, "sc", ld_q, ld_v)
                if tb == 0:
                    W(sc, "sc", ev, aneg_ev)
                else:
                    W(sc, "sc", ev, p0_ev_v(tb - 1, 4))
                v_ = sea.bump(sc.activation(out=p0_sq[:], in_=p0_x[tb % 2][:],
                                            func=AF.Square, accum_out=smt[:, 1:2]))
                W(sc, "sc", sea, v_)
                ea.bump(sc.activation(out=smt[:, 2:3], in_=smt[:, 1:2],
                                      func=AF.Copy, scale=1.0 / H))
                assert ea.n == p0_ea_v(tb, 1)

            @block.vector
            def _(v, tb=tb):
                W(v, "v", ea, p0_ea_v(tb, 1))
                W(v, "v", sev, sev.n)
                ev.bump(v.scalar_tensor_tensor(smt[:, 3:4], mean_c[:, tb:tb + 1],
                                               mean_c[:, tb:tb + 1], smt[:, 2:3],
                                               op0=ALU.mult, op1=ALU.subtract))
                assert ev.n == p0_ev_v(tb, 2)

            @block.scalar
            def _(sc, tb=tb):
                W(sc, "sc", ev, p0_ev_v(tb, 2))
                v_ = sea.bump(sc.activation(out=smt[:, 6:7], in_=smt[:, 3:4],
                                            func=AF.Ln, scale=-1.0, bias=EPS))
                W(sc, "sc", sea, v_)
                ea.bump(sc.activation(out=rstd_c[:, tb:tb + 1], in_=smt[:, 6:7],
                                      func=AF.Exp, scale=-0.5))
                assert ea.n == p0_ea_v(tb, 2)

            @block.vector
            def _(v, tb=tb):
                W(v, "v", ea, p0_ea_v(tb, 2))
                W(v, "v", sev, sev.n)
                ev.bump(v.scalar_tensor_tensor(nmr_c[:, tb:tb + 1],
                                               mean_c[:, tb:tb + 1], -1.0,
                                               rstd_c[:, tb:tb + 1],
                                               op0=ALU.mult, op1=ALU.mult))
                assert ev.n == p0_ev_v(tb, 3)

            @block.scalar
            def _(sc, tb=tb):
                W(sc, "sc", ev, p0_ev_v(tb, 3))
                v_ = sea.bump(sc.activation(out=p0_xn[:], in_=p0_x[tb % 2][:],
                                            func=AF.Identity,
                                            scale=rstd_c[:, tb:tb + 1],
                                            bias=nmr_c[:, tb:tb + 1]))
                W(sc, "sc", sea, v_)
                v_ = sea.bump(sc.activation(out=p0_sq[:], in_=p0_xn[:],
                                            func=AF.Square,
                                            accum_out=smt[:, 9:10]))
                W(sc, "sc", sea, v_)
                v_ = sea.bump(sc.activation(out=smt[:, 10:11], in_=smt[:, 9:10],
                                            func=AF.Ln, scale=1.0 / H, bias=EPS))
                W(sc, "sc", sea, v_)
                ea.bump(sc.activation(out=smt[:, 11:12], in_=smt[:, 10:11],
                                      func=AF.Exp, scale=-0.5))
                assert ea.n == p0_ea_v(tb, 3)

            @block.vector
            def _(v, tb=tb):
                W(v, "v", ea, p0_ea_v(tb, 3))
                if tb >= 1:
                    W(v, "v", mm, p0_mm.get((tb - 1, HB - 1), 0))
                ev.bump(v.tensor_scalar_mul(p0_rtm[:], p0_xn[:], smt[:, 11:12]))
                assert ev.n == p0_ev_v(tb, 4)

            @block.tensor
            def _(te, tb=tb):
                if tb == 0:
                    W(te, "te", qc, const_qx)
                W(te, "te", ev, p0_ev_v(tb, 4))
                for hb in range(HB):
                    if tb > 0 or hb >= 2:
                        ptb, phb = (tb, hb - 2) if hb >= 2 else (tb - 1, HB - 2 + hb)
                        W(te, "te", ea, p0_ea_v(ptb, 4 + phb))
                    mm.bump(te.transpose(psTR[hb % 2][:],
                                         p0_rtm[:, hb * 128:(hb + 1) * 128],
                                         ident[:]))
                    p0_mm[(tb, hb)] = mm.n

            @block.scalar
            def _(sc, tb=tb):
                for hb in range(HB):
                    W(sc, "sc", mm, p0_mm[(tb, hb)])
                    ea.bump(sc.copy(
                        out=AR1[:, hb * 2048 + tb * 128: hb * 2048 + tb * 128 + 128],
                        in_=psTR[hb % 2][:]))
                    assert ea.n == p0_ea_v(tb, 4 + hb)

        if "rmsn" in taps:
            @block.sync
            def _(sy):
                W(sy, "sy", ea, p0_ea_v(TB - 1, 19))
                sy.dma_start(out=tap_t["rmsn"], in_=AR1[:]).then_inc(qs[3].h, 16)
                qs[3].n += 16

        take_snap()
        psP = [ppsum.enter_context(nc.psum_tensor("psP%d" % i, [128, 512], F32))
               for i in range(2)]
        mm_p1 = mm.n
        ea_p1 = ea.n
        qw_p1 = [qw_[0].n, qw_[1].n]
        qs_p1 = [q.n for q in qs]

        def p1_mm_tr(ob, hb):
            return mm_p1 + 20 * ob + hb + 1

        def p1_mm_ch(ob, t5):
            return mm_p1 + 20 * ob + 16 + t5 + 1

        def p1_ea_tr(ob, hb):
            return ea_p1 + 20 * ob + hb + 1

        def p1_ea_ev(ob, t5):
            return ea_p1 + 20 * ob + 16 + t5 + 1

        @block.gpsimd
        def _(gp):
            gate(gp, "gp")
            for ob in range(32):
                if ob >= 2:
                    W(gp, "gp", mm, p1_mm_tr(ob - 2, HB - 1))
                q = qw_[ob % 2]
                gp.dma_start(out=p1_row[ob % 2][:],
                             in_=ipw[ob * 128:(ob + 1) * 128, :]).then_inc(q.h, 16)
                q.n += 16

        @block.tensor
        def _(te):
            gate(te, "te")
            for ob in range(32):
                W(te, "te", qw_[ob % 2], qw_p1[ob % 2] + 16 * (ob // 2 + 1))
                for hb in range(HB):
                    gi = 16 * ob + hb
                    if gi >= 2:
                        pob, phb = divmod(gi - 2, HB)
                        W(te, "te", ea, p1_ea_tr(pob, phb) if pob * 20 >= 0 else 0)
                    mm.bump(te.transpose(psTR[hb % 2][:],
                                         p1_row[ob % 2][:, hb * 128:(hb + 1) * 128],
                                         ident[:]))
                    assert mm.n == p1_mm_tr(ob, hb)
                W(te, "te", ea, p1_ea_tr(ob, HB - 1))
                for t5 in range(NT5):
                    ci = 4 * ob + t5
                    if ci >= 2:
                        pob, pt5 = divmod(ci - 2, NT5)
                        W(te, "te", ea, p1_ea_ev(pob, pt5))
                    for hb in range(HB):
                        inst = te.matmul(
                            psP[ci % 2][:],
                            p1_ipt[ob % 2][:, hb * 128:(hb + 1) * 128],
                            AR1[:, hb * 2048 + t5 * 512: hb * 2048 + t5 * 512 + 512],
                            start=(hb == 0), stop=(hb == HB - 1))
                    mm.bump(inst)
                    assert mm.n == p1_mm_ch(ob, t5)

        @block.scalar
        def _(sc):
            gate(sc, "sc")
            for ob in range(32):
                if ob >= 2:
                    W(sc, "sc", mm, p1_mm_ch(ob - 2, NT5 - 1))
                for hb in range(HB):
                    W(sc, "sc", mm, p1_mm_tr(ob, hb))
                    ea.bump(sc.activation(
                        out=p1_ipt[ob % 2][:, hb * 128:(hb + 1) * 128],
                        in_=psTR[hb % 2][:], func=AF.Identity,
                        scale=w_c[:, hb:hb + 1]))
                    assert ea.n == p1_ea_tr(ob, hb)
                for t5 in range(NT5):
                    ci = 4 * ob + t5
                    W(sc, "sc", mm, p1_mm_ch(ob, t5))
                    if ob < HB:
                        inst = sc.copy(
                            out=AR2[:, ob * 2048 + t5 * 512:
                                    ob * 2048 + t5 * 512 + 512],
                            in_=psP[ci % 2][:])
                    else:
                        zi = ci - 64
                        if zi >= 4:
                            W(sc, "sc", qs[zi % 4],
                              qs_p1[zi % 4] + 16 * ((zi - 4) // 4 + 1))
                        slot = zi % 4
                        inst = sc.copy(out=stbf[:, slot * 512:(slot + 1) * 512],
                                       in_=psP[ci % 2][:])
                    ea.bump(inst)
                    assert ea.n == p1_ea_ev(ob, t5)

        @block.sync
        def _(sy):
            for ob in range(HB, 32):
                for t5 in range(NT5):
                    zi = 4 * (ob - HB) + t5
                    W(sy, "sy", ea, p1_ea_ev(ob, t5))
                    slot = zi % 4
                    q = qs[slot]
                    sy.dma_start(
                        out=z_sp[(ob - HB) * 128:(ob - HB + 1) * 128,
                                 t5 * 512:(t5 + 1) * 512],
                        in_=stbf[:, slot * 512:(slot + 1) * 512]).then_inc(q.h, 16)
                    q.n += 16
                    assert q.n == qs_p1[slot] + 16 * (zi // 4 + 1)

        if "c1" in taps:
            @block.sync
            def _(sy):
                W(sy, "sy", ea, p1_ea_ev(HB - 1, NT5 - 1))
                sy.dma_start(out=tap_t["c1"], in_=AR2[:]).then_inc(qs[3].h, 16)
                qs[3].n += 16
        if stage < 2:
            _finish(nc, block, W, gate, qs, ea, y_out, p7_o, AR2, stbf, take_snap)
            return nc

        take_snap()
        ppsum.close()
        ppsum = ExitStack()
        psC2 = [ppsum.enter_context(nc.psum_tensor("psC2%d" % i, [128, 512], F32))
                for i in range(4)]
        mm_p2 = mm.n
        ea_p2 = ea.n
        qw_p2 = [qw_[0].n, qw_[1].n]

        def p2_mm(di):
            return mm_p2 + di + 1

        def p2_ea(di):
            return ea_p2 + di + 1

        @block.gpsimd
        def _(gp):
            gate(gp, "gp")
            for ob in range(HB):
                if ob >= 2:
                    W(gp, "gp", mm, p2_mm(4 * (ob - 1)))
                src = convw[:, :, ob * 128:(ob + 1) * 128].rearrange(
                    "k (ib p) o -> p k ib o", p=128)
                q = qw_[ob % 2]
                gp.dma_start(out=p2_cw[ob % 2][:].rearrange(
                    "p (k ib o) -> p k ib o", k=4, ib=16), in_=src).then_inc(q.h, 16)
                q.n += 16

        @block.tensor
        def _(te):
            gate(te, "te")
            for ob in range(HB):
                W(te, "te", qw_[ob % 2], qw_p2[ob % 2] + 16 * (ob // 2 + 1))
                for t5 in range(NT5):
                    di = 4 * ob + t5
                    if di >= 4:
                        W(te, "te", ea, p2_ea(di - 4))
                    bank = psC2[di % 4]
                    first = True
                    for k in (3, 2, 1, 0):
                        shift = k - 3
                        for ib in range(HB):
                            lhs = p2_cw[ob % 2][:, (k * 16 + ib) * 128:
                                                (k * 16 + ib) * 128 + 128]
                            t0 = t5 * 512 + shift
                            if t0 >= 0:
                                rhs = AR2[:, ib * 2048 + t0: ib * 2048 + t0 + 512]
                                outp = bank[:]
                            else:
                                cut = -t0
                                rhs = AR2[:, ib * 2048: ib * 2048 + 512 - cut]
                                outp = bank[:, cut:512]
                            inst = te.matmul(outp, lhs, rhs, start=first,
                                             stop=(k == 0 and ib == HB - 1))
                            first = False
                    mm.bump(inst)
                    assert mm.n == p2_mm(di)

        @block.scalar
        def _(sc):
            gate(sc, "sc")
            for di in range(4 * HB):
                ob, t5 = divmod(di, NT5)
                W(sc, "sc", mm, p2_mm(di))
                ea.bump(sc.activation(
                    out=AR1[:, ob * 2048 + t5 * 512: ob * 2048 + t5 * 512 + 512],
                    in_=psC2[di % 4][:], func=AF.Silu, bias=cb_c[:, ob:ob + 1]))
                assert ea.n == p2_ea(di)

        if "xc" in taps:
            @block.sync
            def _(sy):
                W(sy, "sy", ea, p2_ea(4 * HB - 1))
                sy.dma_start(out=tap_t["xc"], in_=AR1[:]).then_inc(qs[3].h, 16)
                qs[3].n += 16
        if stage < 3:
            _finish(nc, block, W, gate, qs, ea, y_out, p7_o, AR1, stbf, take_snap)
            return nc

        take_snap()
        ppsum.close()
        ppsum = ExitStack()
        psTR = [ppsum.enter_context(nc.psum_tensor("psTR3%d" % i, [128, 128], BF16))
                for i in range(2)]
        mm_pp = mm.n
        ea_pp = ea.n
        qw_pp = [qw_[0].n, qw_[1].n]

        @block.gpsimd
        def _(gp):
            gate(gp, "gp")
            for ob in range(HB):
                if ob >= 2:
                    W(gp, "gp", mm, mm_pp + 16 * (ob - 1))
                q = qw_[ob % 2]
                gp.dma_start(out=p3_oprow[ob % 2][:],
                             in_=opw[ob * 128:(ob + 1) * 128, :]).then_inc(q.h, 16)
                q.n += 16
            gp.dma_start(out=p3_xpr[:], in_=xpw).then_inc(qd[0].h, 16)
            qd[0].n += 16
            gp.dma_start(out=p3_dtr[:].rearrange("p (hb r) -> p hb r", r=64),
                         in_=dtw.rearrange("(hb p) r -> p hb r", p=128)
                         ).then_inc(qd[1].h, 16)
            qd[1].n += 16

        @block.tensor
        def _(te):
            gate(te, "te")
            for ob in range(HB):
                W(te, "te", qw_[ob % 2], qw_pp[ob % 2] + 16 * (ob // 2 + 1))
                for hb in range(HB):
                    ti = ob * HB + hb
                    if ti >= 2:
                        W(te, "te", ea, ea_pp + ti - 1)
                    mm.bump(te.transpose(psTR[ti % 2][:],
                                         p3_oprow[ob % 2][:, hb * 128:(hb + 1) * 128],
                                         ident[:]))
            W(te, "te", qd[0], qd[0].n)
            for hb in range(HB):
                ti = 256 + hb
                W(te, "te", ea, ea_pp + ti - 1)
                mm.bump(te.transpose(psTR[ti % 2][:, 0:96],
                                     p3_xpr[:, hb * 128:(hb + 1) * 128],
                                     ident[0:96, 0:96]))
            W(te, "te", qd[1], qd[1].n)
            for hb in range(HB):
                ti = 272 + hb
                W(te, "te", ea, ea_pp + ti - 1)
                mm.bump(te.transpose(psTR[ti % 2][0:64, :],
                                     p3_dtr[:, hb * 64:(hb + 1) * 64], ident[:]))

        @block.scalar
        def _(sc):
            gate(sc, "sc")
            for ob in range(HB):
                for hb in range(HB):
                    ti = ob * HB + hb
                    W(sc, "sc", mm, mm_pp + ti + 1)
                    ea.bump(sc.copy(
                        out=AR2[:, hb * 2048 + ob * 128: hb * 2048 + ob * 128 + 128],
                        in_=psTR[ti % 2][:]))
            for hb in range(HB):
                ti = 256 + hb
                W(sc, "sc", mm, mm_pp + ti + 1)
                ea.bump(sc.copy(out=xpwT[:, hb * 96:(hb + 1) * 96],
                                in_=psTR[ti % 2][:, 0:96]))
            for hb in range(HB):
                ti = 272 + hb
                W(sc, "sc", mm, mm_pp + ti + 1)
                ea.bump(sc.copy(out=dtwT[:, hb * 128:(hb + 1) * 128],
                                in_=psTR[ti % 2][0:64, :]))

        @block.vector
        def _(v):
            ev.bump(v.memset(stat[:], 0.0))

        take_snap()
        ppsum.close()
        ppsum = ExitStack()
        pe_ = ppsum.enter_context
        psA = pe_(nc.psum_tensor("psA", [64, 256], F32))
        psB = pe_(nc.psum_tensor("psB", [16, 256], F32))
        psC = pe_(nc.psum_tensor("psC", [16, 256], F32))
        psD3 = [pe_(nc.psum_tensor("psD3%d" % i, [128, 256], F32)) for i in range(2)]
        psBC = [pe_(nc.psum_tensor("psBC%d" % i, [128, 256], F32)) for i in range(2)]
        psO = pe_(nc.psum_tensor("psO", [128, 512], F32))

        mm3 = mm.n
        ea3 = ea.n
        ev3 = ev.n
        qa3 = qa[0].n
        qs3 = [qs[0].n, qs[1].n]
        MM_CH = 59
        EA_CH = 315
        EV_CH = 288

        def v_mm(c, off):
            return mm3 + MM_CH * c + off

        def v_ea(c, off):
            return ea3 + EA_CH * c + off

        def v_ev(c, off):
            return ev3 + EV_CH * c + off

        def v_dfin(c, hb):
            return v_ev(c, 18 * hb + 1)

        def v_scan(c, gl):
            hb, n = divmod(gl, NST)
            return v_ev(c, 18 * hb + 2 + n)

        def v_qs3(c, d):
            return qs3[d % 2] + 16 * (4 * c + d // 2 + 1)

        zrdy = {}

        @block.sync
        def _(sy):
            gate(sy, "sy")
            src = z_sp[:, 0:TC].rearrange("(hb p) t -> p hb t", p=128)
            sy.dma_start(out=p3_z[:].rearrange("p (hb t) -> p hb t", t=TC),
                         in_=src).then_inc(qa[0].h, 16)
            qa[0].n += 16
            zrdy[0] = qa[0].n

        for c in range(NCH):
            cT = c * TC

            @block.tensor
            def _(te, c=c):
                if c == 0:
                    gate(te, "te")
                if c > 0:
                    W(te, "te", ea, v_ea(c - 1, 3))
                for part, ps, lo, hi in ((1, psA, 0, 64), (2, psB, 64, 80),
                                         (3, psC, 80, 96)):
                    for hb in range(HB):
                        inst = te.matmul(
                            ps[:], xpwT[:, hb * 96 + lo: hb * 96 + hi],
                            AR1[:, hb * 2048 + cT: hb * 2048 + cT + TC],
                            start=(hb == 0), stop=(hb == HB - 1))
                    mm.bump(inst)
                    assert mm.n == v_mm(c, part)
                W(te, "te", ea, v_ea(c, 3))
                for n in range(NST):
                    for half in (0, 1):
                        if n >= 1:
                            W(te, "te", ea, v_ea(c, 4 + 2 * (n - 1) + half))
                        elif c > 0:
                            W(te, "te", ea, v_ea(c - 1, 4 + 30 + half))
                        mm.bump(te.matmul(psBC[half][:],
                                          sel[:, n * 128:(n + 1) * 128],
                                          bmcm[:, half * 256:half * 256 + TC],
                                          start=True, stop=True))
                        assert mm.n == v_mm(c, 4 + 2 * n + half)
                W(te, "te", ea, v_ea(c, 2))
                for hb in range(HB):
                    if hb >= 2:
                        W(te, "te", ea, v_ea(c, 36 + 17 * (hb - 2)))
                    elif c > 0:
                        W(te, "te", ea, v_ea(c - 1, 36 + 17 * (14 + hb)))
                    mm.bump(te.matmul(psD3[hb % 2][:],
                                      dtwT[:, hb * 128:(hb + 1) * 128],
                                      din[:], start=True, stop=True))
                    assert mm.n == v_mm(c, 36 + hb)

            @block.scalar
            def _(sc, c=c):
                if c == 0:
                    gate(sc, "sc")
                W(sc, "sc", qa[0], zrdy[c])
                if c > 0:
                    W(sc, "sc", ev, v_ev(c - 1, EV_CH))
                for hb in range(HB):
                    inst = sc.activation(out=p3_sz[:, hb * 256:(hb + 1) * 256],
                                         in_=p3_z[:, hb * 256:(hb + 1) * 256],
                                         func=AF.Silu)
                ea.bump(inst)
                assert ea.n == v_ea(c, 1)
                W(sc, "sc", mm, v_mm(c, 1))
                ea.bump(sc.copy(out=din[:], in_=psA[:]))
                W(sc, "sc", mm, v_mm(c, 3))
                sc.copy(out=bmcm[:, 0:TC], in_=psB[:])
                ea.bump(sc.copy(out=bmcm[:, 256:256 + TC], in_=psC[:]))
                assert ea.n == v_ea(c, 3)
                for n in range(NST):
                    for half in (0, 1):
                        W(sc, "sc", mm, v_mm(c, 4 + 2 * n + half))
                        ea.bump(sc.copy(
                            out=p3_bc[:, (2 * n + half) * 256:
                                      (2 * n + half) * 256 + TC],
                            in_=psBC[half][:]))
                        assert ea.n == v_ea(c, 4 + 2 * n + half)
                for hb in range(HB):
                    W(sc, "sc", mm, v_mm(c, 36 + hb))
                    gl = c * HB + hb
                    if gl >= 2:
                        pc, ph = divmod(gl - 2, HB)
                        W(sc, "sc", ev, v_dfin(pc, ph))
                    sc.activation(out=p3_xs[hb % 2][:], in_=psD3[hb % 2][:],
                                  func=AF.Identity, bias=dtb_c[:, hb:hb + 1])
                    v_ = sea.bump(sc.activation(out=p3_tl[hb % 2][:],
                                                in_=psD3[hb % 2][:],
                                                func=AF.Abs,
                                                bias=dtb_c[:, hb:hb + 1]))
                    W(sc, "sc", sea, v_)
                    v_ = sea.bump(sc.activation(out=p3_tl[hb % 2][:],
                                                in_=p3_tl[hb % 2][:],
                                                func=AF.Exp, scale=-1.0))
                    W(sc, "sc", sea, v_)
                    ea.bump(sc.activation(out=p3_tl[hb % 2][:],
                                          in_=p3_tl[hb % 2][:],
                                          func=AF.Ln, bias=1.0))
                    assert ea.n == v_ea(c, 36 + 17 * hb)
                    W(sc, "sc", ev, v_dfin(c, hb))
                    for n in range(NST):
                        gl2 = hb * NST + n
                        if gl2 >= 3:
                            W(sc, "sc", ev, v_scan(c, gl2 - 3))
                        ea.bump(sc.activation(
                            out=p3_a[gl2 % 3][:], in_=p3_dl[hb % 4][:],
                            func=AF.Exp,
                            scale=a_c[:, hb * 16 + n: hb * 16 + n + 1]))
                        assert ea.n == v_ea(c, 37 + 17 * hb + n)

            @block.vector
            def _(v, c=c):
                if c == 0:
                    gate(v, "v")
                if c > 0:
                    W(v, "v", ev, v_ev(c - 1, EV_CH))
                if c > 0:
                    W(v, "v", mm, v_mm(c - 1, 59))
                for hb in range(HB):
                    W(v, "v", ea, v_ea(c, 36 + 17 * hb))
                    gl = c * HB + hb
                    if gl >= 4:
                        pc, ph = divmod(gl - 4, HB)
                        W(v, "v", ea, v_ea(pc, 37 + 17 * ph + NST - 1))
                    ev.bump(v.scalar_tensor_tensor(p3_dl[hb % 4][:],
                                                   p3_xs[hb % 2][:], 0.0,
                                                   p3_tl[hb % 2][:],
                                                   op0=ALU.max, op1=ALU.add))
                    assert ev.n == v_dfin(c, hb)
                    v.tensor_mul(p3_du[:], p3_dl[hb % 4][:],
                                 AR1[:, hb * 2048 + cT: hb * 2048 + cT + TC])
                    for n in range(NST):
                        gl2 = hb * NST + n
                        W(v, "v", ea, v_ea(c, 37 + 17 * hb + n))
                        W(v, "v", ea, v_ea(c, 4 + 2 * n + 1))
                        hs_sl = p3_hs4[:, (n % 4) * 256:(n % 4) * 256 + TC]
                        v.tensor_mul(p3_y[:], p3_du[:],
                                     p3_bc[:, (2 * n) * 256:(2 * n) * 256 + TC])
                        init = 0.0 if c == 0 else \
                            stat[:, hb * 16 + n: hb * 16 + n + 1]
                        ev.bump(v.tensor_tensor_scan(hs_sl, p3_a[gl2 % 3][:],
                                                     p3_y[:], init,
                                                     op0=ALU.mult, op1=ALU.add))
                        assert ev.n == v_scan(c, gl2)
                        if n % 4 == 3 and c < NCH - 1:
                            v.tensor_copy(
                                stat[:, hb * 16 + n - 3: hb * 16 + n + 1]
                                .rearrange("p (n o) -> p n o", o=1),
                                p3_hs4[:].rearrange("p (n s) -> p n s", s=256)
                                [:, :, TC - 1:TC])
                        if n == 0:
                            v.tensor_mul(p3_ya[:], hs_sl,
                                         p3_bc[:, (2 * n + 1) * 256:
                                               (2 * n + 1) * 256 + TC])
                        else:
                            v.tensor_mul(p3_y2[:], hs_sl,
                                         p3_bc[:, (2 * n + 1) * 256:
                                               (2 * n + 1) * 256 + TC])
                            v.tensor_add(p3_ya[:], p3_ya[:], p3_y2[:])
                    v.scalar_tensor_tensor(p3_y[:],
                                           AR1[:, hb * 2048 + cT:
                                               hb * 2048 + cT + TC],
                                           d_c[:, hb:hb + 1],
                                           p3_ya[:], op0=ALU.mult, op1=ALU.add)
                    W(v, "v", ea, v_ea(c, 1))
                    ev.bump(v.tensor_mul(p3_yg[:, hb * 256:(hb + 1) * 256],
                                         p3_y[:],
                                         p3_sz[:, hb * 256:(hb + 1) * 256]))
                    assert ev.n == v_ev(c, 18 * hb + 18)

            @block.tensor
            def _(te, c=c):
                W(te, "te", ev, v_ev(c, EV_CH))
                for d in range(8):
                    toff, h5 = divmod(d, 4)
                    if d > 0:
                        W(te, "te", ea, v_ea(c, 308 + d - 1))
                    elif c > 0:
                        W(te, "te", ea, v_ea(c - 1, 315))
                    for ob in range(HB):
                        inst = te.matmul(
                            psO[:],
                            p3_yg[:, ob * 256 + toff * 128:
                                  ob * 256 + toff * 128 + 128],
                            AR2[:, ob * 2048 + h5 * 512: ob * 2048 + h5 * 512 + 512],
                            start=(ob == 0), stop=(ob == HB - 1))
                    mm.bump(inst)
                    assert mm.n == v_mm(c, 52 + d)

            @block.scalar
            def _(sc, c=c):
                for d in range(8):
                    W(sc, "sc", mm, v_mm(c, 52 + d))
                    gd = c * 8 + d
                    if gd >= 2:
                        pc, pd = divmod(gd - 2, 8)
                        W(sc, "sc", qs[pd % 2], v_qs3(pc, pd))
                    slot = d % 2
                    ea.bump(sc.copy(out=stbf[:, slot * 512:(slot + 1) * 512],
                                    in_=psO[:]))
                    assert ea.n == v_ea(c, 308 + d)

            @block.sync
            def _(sy, c=c):
                for d in range(8):
                    toff, h5 = divmod(d, 4)
                    W(sy, "sy", ea, v_ea(c, 308 + d))
                    slot = d % 2
                    q = qs[slot]
                    sy.dma_start(
                        out=mo_sp[cT + toff * 128: cT + toff * 128 + 128,
                                  h5 * 512:(h5 + 1) * 512],
                        in_=stbf[:, slot * 512:(slot + 1) * 512]).then_inc(q.h, 16)
                    q.n += 16
                    assert q.n == v_qs3(c, d)
                if c + 1 < NCH:
                    W(sy, "sy", ea, v_ea(c, 1))
                    src = z_sp[:, (c + 1) * TC:(c + 2) * TC].rearrange(
                        "(hb p) t -> p hb t", p=128)
                    sy.dma_start(out=p3_z[:].rearrange("p (hb t) -> p hb t", t=TC),
                                 in_=src).then_inc(qa[0].h, 16)
                    qa[0].n += 16
                    zrdy[c + 1] = qa[0].n

        if stage < 4:
            _finish(nc, block, W, gate, qs, ea, y_out, p7_o, AR1, stbf, take_snap)
            return nc

        take_snap()
        ppsum.close()
        ppsum = ExitStack()
        psTR = [ppsum.enter_context(nc.psum_tensor("psTR4%d" % i, [128, 128], BF16))
                for i in range(2)]
        psL = ppsum.enter_context(nc.psum_tensor("psL", [128, 32], F32))
        psS = ppsum.enter_context(nc.psum_tensor("psS", [128, 16], F32))
        ea4 = ea.n
        ev4 = ev.n
        EA4 = 18
        EV4 = 3

        def p4_ea(tb, off):
            return ea4 + EA4 * tb + off

        def p4_ev(tb, off):
            return ev4 + EV4 * tb + off

        p4_mm = {}

        p4_ld = []

        @block.sync
        def _(sy):
            gate(sy, "sy")
            for tb in range(TB):
                if tb >= 2:
                    W(sy, "sy", ev, p4_ev(tb - 2, 1))
                    W(sy, "sy", ea, p4_ea(tb - 2, 1))
                sy.dma_start(out=p4_x[tb % 2][:],
                             in_=x_in[tb * 128:(tb + 1) * 128, :]
                             ).then_inc(qa[tb % 2].h, 16)
                qa[tb % 2].n += 16
                sy.dma_start(out=p4_mo[tb % 2][:],
                             in_=mo_sp[tb * 128:(tb + 1) * 128, :]
                             ).then_inc(qb[tb % 2].h, 16)
                qb[tb % 2].n += 16
                p4_ld.append((qa[tb % 2].n, qb[tb % 2].n))

        for tb in range(TB):
            @block.scalar
            def _(sc, tb=tb):
                if tb == 0:
                    gate(sc, "sc")
                W(sc, "sc", qa[tb % 2], p4_ld[tb][0])
                if tb >= 1:
                    W(sc, "sc", ev, p4_ev(tb - 1, 1))
                    W(sc, "sc", mm, p4_mm.get((tb - 1, HB - 1), 0))
                v_ = sea.bump(sc.activation(out=p4_xn[:], in_=p4_x[tb % 2][:],
                                            func=AF.Identity,
                                            scale=rstd_c[:, tb:tb + 1],
                                            bias=nmr_c[:, tb:tb + 1]))
                W(sc, "sc", sea, v_)
                ea.bump(sc.copy(out=stbf[:], in_=p4_xn[:]))
                assert ea.n == p4_ea(tb, 1)

            @block.vector
            def _(v, tb=tb):
                if tb == 0:
                    gate(v, "v")
                W(v, "v", ea, p4_ea(tb, 1))
                W(v, "v", qb[tb % 2], p4_ld[tb][1])
                v.tensor_add(p4_t1[:], p4_x[tb % 2][:], p4_xn[:])
                ev.bump(v.tensor_add(AR2[:, tb * 2048:(tb + 1) * 2048],
                                     p4_t1[:], p4_mo[tb % 2][:]))
                assert ev.n == p4_ev(tb, 1)

            @block.tensor
            def _(te, tb=tb):
                if tb == 0:
                    gate(te, "te")
                W(te, "te", ea, p4_ea(tb, 1))
                for hb in range(HB):
                    if tb > 0 or hb >= 2:
                        ptb, phb = (tb, hb - 2) if hb >= 2 else (tb - 1, HB - 2 + hb)
                        W(te, "te", ea, p4_ea(ptb, 2 + phb))
                    mm.bump(te.transpose(psTR[hb % 2][:],
                                         stbf[:, hb * 128:(hb + 1) * 128],
                                         ident[:]))
                    p4_mm[(tb, hb)] = mm.n

            @block.scalar
            def _(sc, tb=tb):
                for hb in range(HB):
                    W(sc, "sc", mm, p4_mm[(tb, hb)])
                    ea.bump(sc.copy(
                        out=AR1[:, hb * 2048 + tb * 128: hb * 2048 + tb * 128 + 128],
                        in_=psTR[hb % 2][:]))
                    assert ea.n == p4_ea(tb, 2 + hb)

            @block.tensor
            def _(te, tb=tb):
                W(te, "te", ea, p4_ea(tb, 17))
                if tb >= 1:
                    W(te, "te", ea, p4_ea(tb - 1, 18))
                for hb in range(HB):
                    inst = te.matmul(psL[:],
                                     AR1[:, hb * 2048 + tb * 128:
                                         hb * 2048 + tb * 128 + 128],
                                     gateT[:, hb * 32:(hb + 1) * 32],
                                     start=(hb == 0), stop=(hb == HB - 1))
                mm.bump(inst)
                p4_mm[("lg", tb)] = mm.n
                for hb in range(HB):
                    inst = te.matmul(psS[:, tb:tb + 1],
                                     AR1[:, hb * 2048 + tb * 128:
                                         hb * 2048 + tb * 128 + 128],
                                     segT[:, hb:hb + 1],
                                     start=(hb == 0), stop=(hb == HB - 1))
                mm.bump(inst)
                p4_mm[("sg", tb)] = mm.n

            @block.vector
            def _(v, tb=tb):
                W(v, "v", mm, p4_mm[("lg", tb)])
                ev.bump(v.reduce_max(smt[:, 16:17], psL[:], axis=AX.X, negate=True))
                assert ev.n == p4_ev(tb, 2)

            @block.scalar
            def _(sc, tb=tb):
                W(sc, "sc", ev, p4_ev(tb, 2))
                ea.bump(sc.activation(out=smt[:, 32:64], in_=psL[:], func=AF.Exp,
                                      bias=smt[:, 16:17]))
                assert ea.n == p4_ea(tb, 18)

            @block.vector
            def _(v, tb=tb):
                W(v, "v", ea, p4_ea(tb, 18))
                ops = [
                    lambda: v.reduce_sum(smt[:, 17:18], smt[:, 32:64], axis=AX.X),
                    lambda: v.reciprocal(smt[:, 18:19], smt[:, 17:18]),
                    lambda: v.tensor_scalar_mul(smt[:, 32:64], smt[:, 32:64],
                                                smt[:, 18:19]),
                    lambda: v.max(out=mx8[:], in_=smt[:, 32:64]),
                    lambda: v.tensor_single_scalar(comb[:, tb * 32:(tb + 1) * 32],
                                                   smt[:, 32:64], mx8[:, 3:4],
                                                   op=ALU.is_ge),
                    lambda: v.tensor_mul(comb[:, tb * 32:(tb + 1) * 32],
                                         comb[:, tb * 32:(tb + 1) * 32],
                                         smt[:, 32:64]),
                    lambda: v.reduce_sum(smt[:, 19:20],
                                         comb[:, tb * 32:(tb + 1) * 32],
                                         axis=AX.X),
                    lambda: v.reciprocal(smt[:, 20:21], smt[:, 19:20]),
                ]
                for op in ops:
                    v_ = sev.bump(op())
                    W(v, "v", sev, v_)
                ev.bump(v.tensor_scalar_mul(comb[:, tb * 32:(tb + 1) * 32],
                                            comb[:, tb * 32:(tb + 1) * 32],
                                            smt[:, 20:21]))
                assert ev.n == p4_ev(tb, 3)

        @block.scalar
        def _(sc):
            W(sc, "sc", mm, p4_mm[("sg", TB - 1)])
            ea.bump(sc.activation(out=s_c[:], in_=psS[:], func=AF.Sigmoid))

        if "comb" in taps:
            @block.vector
            def _(v):
                ev.bump(v.tensor_copy(AR1[:, 0:512], comb[:]))

            @block.sync
            def _(sy):
                W(sy, "sy", ev)
                sy.dma_start(out=tap_t["comb"][:, 0:512],
                             in_=AR1[:, 0:512]).then_inc(qs[3].h, 16)
                qs[3].n += 16
        if stage < 5:
            _finish(nc, block, W, gate, qs, ea, y_out, p7_o, AR2, stbf, take_snap)
            return nc

        take_snap()
        ppsum.close()
        ppsum = ExitStack()
        psG = [ppsum.enter_context(nc.psum_tensor("psG%d" % i, [128, 512], F32))
               for i in range(2)]
        psU = [ppsum.enter_context(nc.psum_tensor("psU%d" % i, [128, 512], F32))
               for i in range(2)]
        psD5 = [ppsum.enter_context(nc.psum_tensor("psD5%d" % i, [128, 512], F32))
                for i in range(4)]

        st5 = {"mm_wl": {}, "mm_dl": {}}

        def expert_phase(wgw, wuw, wdw, nfb, wgt_c, wut_c, wdt_b, hbuf, combcol):
            mmE = mm.n
            eaE = ea.n
            evE = ev.n
            qgE = [qg[0].n, qg[1].n]
            quE = [qu[0].n, qu[1].n]
            qdE = [qd[0].n, qd[1].n]
            nup = nfb * NT5

            def m_g(ui):
                return mmE + 2 * ui + 1

            def m_u(ui):
                return mmE + 2 * ui + 2

            def m_d(di):
                return mmE + 2 * nup + di + 1

            def a_sg(ui):
                return eaE + ui + 1

            def v_he(ui):
                return evE + ui + 1

            def v_de(di):
                return evE + nup + di + 1

            @block.gpsimd
            def _(gp):
                for f in range(nfb):
                    pv = st5["mm_wl"].get(f % 2, 0)
                    if pv:
                        W(gp, "gp", mm, pv)
                    src = wgw.rearrange("(hb p) (fb f) -> fb hb p f",
                                        p=128, f=128)[f].transpose([1, 0, 2])
                    gp.dma_start(out=wgt_c[f % 2][:].rearrange(
                        "p (hb f) -> p hb f", f=128), in_=src
                        ).then_inc(qg[f % 2].h, 16)
                    qg[f % 2].n += 16
                    src = wuw.rearrange("(hb p) (fb f) -> fb hb p f",
                                        p=128, f=128)[f].transpose([1, 0, 2])
                    gp.dma_start(out=wut_c[f % 2][:].rearrange(
                        "p (hb f) -> p hb f", f=128), in_=src
                        ).then_inc(qu[f % 2].h, 16)
                    qu[f % 2].n += 16
                    st5["mm_wl"][f % 2] = m_u((f + 1) * NT5 - 1)
                for h5 in range(4):
                    pv = st5["mm_dl"].get(h5 % 2, 0)
                    if pv:
                        W(gp, "gp", mm, pv)
                    src = wdw[:, h5 * 512:(h5 + 1) * 512].rearrange(
                        "(fb p) h -> p fb h", p=128)
                    gp.dma_start(out=wdt_b[h5 % 2][:, 0:nfb * 512].rearrange(
                        "p (fb h) -> p fb h", h=512), in_=src
                        ).then_inc(qd[h5 % 2].h, 16)
                    qd[h5 % 2].n += 16
                    st5["mm_dl"][h5 % 2] = m_d((h5 + 1) * TB - 1)

            @block.tensor
            def _(te):
                for f in range(nfb):
                    W(te, "te", qg[f % 2], qgE[f % 2] + 16 * (f // 2 + 1))
                    W(te, "te", qu[f % 2], quE[f % 2] + 16 * (f // 2 + 1))
                    for t5 in range(NT5):
                        ui = f * NT5 + t5
                        if ui >= 2:
                            W(te, "te", ea, a_sg(ui - 2))
                            W(te, "te", ev, v_he(ui - 2))
                        for hb in range(HB):
                            inst = te.matmul(
                                psG[ui % 2][:],
                                wgt_c[f % 2][:, hb * 128:(hb + 1) * 128],
                                AR1[:, hb * 2048 + t5 * 512:
                                    hb * 2048 + t5 * 512 + 512],
                                start=(hb == 0), stop=(hb == HB - 1))
                        mm.bump(inst)
                        assert mm.n == m_g(ui)
                        for hb in range(HB):
                            inst = te.matmul(
                                psU[ui % 2][:],
                                wut_c[f % 2][:, hb * 128:(hb + 1) * 128],
                                AR1[:, hb * 2048 + t5 * 512:
                                    hb * 2048 + t5 * 512 + 512],
                                start=(hb == 0), stop=(hb == HB - 1))
                        mm.bump(inst)
                        assert mm.n == m_u(ui)
                W(te, "te", ev, v_he(nup - 1))
                for h5 in range(4):
                    W(te, "te", qd[h5 % 2], qdE[h5 % 2] + 16 * (h5 // 2 + 1))
                    for tb in range(TB):
                        di = h5 * TB + tb
                        if di >= 4:
                            W(te, "te", ev, v_de(di - 4))
                        for fbk in range(nfb):
                            inst = te.matmul(
                                psD5[di % 4][:],
                                hbuf[:, fbk * 2048 + tb * 128:
                                     fbk * 2048 + tb * 128 + 128],
                                wdt_b[h5 % 2][:, fbk * 512:(fbk + 1) * 512],
                                start=(fbk == 0), stop=(fbk == nfb - 1))
                        mm.bump(inst)
                        assert mm.n == m_d(di)

            @block.scalar
            def _(sc):
                for ui in range(nup):
                    W(sc, "sc", mm, m_g(ui))
                    if ui >= 2:
                        W(sc, "sc", ev, v_he(ui - 2))
                    ea.bump(sc.activation(out=p5_sg[ui % 2][:], in_=psG[ui % 2][:],
                                          func=AF.Silu))
                    assert ea.n == a_sg(ui)

            @block.vector
            def _(v):
                W(v, "v", ev, evE)
                for ui in range(nup):
                    f, t5 = divmod(ui, NT5)
                    W(v, "v", ea, a_sg(ui))
                    W(v, "v", mm, m_u(ui))
                    ev.bump(v.tensor_mul(
                        hbuf[:, f * 2048 + t5 * 512: f * 2048 + t5 * 512 + 512],
                        p5_sg[ui % 2][:], psU[ui % 2][:]))
                    assert ev.n == v_he(ui)
                for di in range(4 * TB):
                    h5, tb = divmod(di, TB)
                    W(v, "v", mm, m_d(di))
                    accs = AR2[:, tb * 2048 + h5 * 512: tb * 2048 + h5 * 512 + 512]
                    ev.bump(v.scalar_tensor_tensor(accs, psD5[di % 4][:],
                                                   combcol(tb), accs,
                                                   op0=ALU.mult, op1=ALU.add))
                    assert ev.n == v_de(di)

        @block.gpsimd
        def _(gp):
            gate(gp, "gp")

        @block.tensor
        def _(te):
            gate(te, "te")

        for ee in range(NEXP):
            expert_phase(wg_w[ee], wu_w[ee], wd_w[ee], 6,
                         p5_wg, p5_wu, p5_wd, p5_he,
                         lambda tb, ee=ee: comb[:, tb * 32 + ee: tb * 32 + ee + 1])

        if stage >= 6:
            take_snap()
            @block.gpsimd
            def _(gp):
                gate(gp, "gp")

            @block.tensor
            def _(te):
                gate(te, "te")
            st5["mm_wl"] = {}
            st5["mm_dl"] = {}
            for fc in range(SF // 512):
                expert_phase(sg_w[:, fc * 512:(fc + 1) * 512],
                             su_w[:, fc * 512:(fc + 1) * 512],
                             sd_w[fc * 512:(fc + 1) * 512, :], 4,
                             p6_wg, p6_wu, p6_wd, p6_hs,
                             lambda tb: s_c[:, tb:tb + 1])

        _finish(nc, block, W, gate, qs, ea, y_out, p7_o, AR2, stbf, take_snap)
    return nc


def _finish(nc, block, W, gate, qs, ea, y_out, p7_o, accbuf, stbf, take_snap):
    take_snap()
    fin = {}
    qsF = [qs[0].n, qs[1].n]

    @block.scalar
    def _(sc):
        gate(sc, "sc")
        for tb in range(TB):
            if tb >= 2:
                W(sc, "sc", qs[tb % 2], qsF[tb % 2] + 16 * ((tb - 2) // 2 + 1))
            ea.bump(sc.copy(out=p7_o[tb % 2][:],
                            in_=accbuf[:, tb * 2048:(tb + 1) * 2048]))
            fin[("e", tb)] = ea.n

    @block.sync
    def _(sy):
        gate(sy, "sy")
        for tb in range(TB):
            W(sy, "sy", ea, fin[("e", tb)])
            q = qs[tb % 2]
            sy.dma_start(out=y_out[tb * 128:(tb + 1) * 128, :],
                         in_=p7_o[tb % 2][:]).then_inc(q.h, 16)
            q.n += 16
        sy.wait_ge(qs[0].h, qs[0].n)
        sy.wait_ge(qs[1].h, qs[1].n)


def kernel(**inputs):
    x = np.ascontiguousarray(np.asarray(inputs["L0_attn_x"], dtype=np.float32))
    shared = {
        "norm_w": inputs["L0_attn_norm_w"],
        "in_proj_w": inputs["L0_attn_in_proj_w"],
        "conv_w": inputs["L0_attn_conv_w"],
        "conv_b": inputs["L0_attn_conv_b"],
        "x_proj_w": inputs["L0_attn_x_proj_w"],
        "dt_proj_w": inputs["L0_attn_dt_proj_w"],
        "dt_proj_b": inputs["L0_attn_dt_proj_b"],
        "A_log": inputs["L0_attn_A_log"],
        "D": inputs["L0_attn_D"],
        "out_proj_w": inputs["L0_attn_out_proj_w"],
        "gate_w": inputs["L0_ffn_gate_w"],
        "wg": inputs["L0_ffn_wg"],
        "wu": inputs["L0_ffn_wu"],
        "wd": inputs["L0_ffn_wd"],
        "sg": inputs["L0_ffn_sg"],
        "su": inputs["L0_ffn_su"],
        "sd": inputs["L0_ffn_sd"],
        "seg": inputs["L0_ffn_seg"],
    }
    shared = {k: np.ascontiguousarray(np.asarray(v, dtype=np.float32))
              for k, v in shared.items()}
    nc = build()
    in_maps = [dict(x=x[c], **shared) for c in range(8)]
    res = run_bass_kernel_spmd(nc, in_maps, core_ids=list(range(8)))
    out = np.stack([np.asarray(res.results[c]["y"]) for c in range(8)], axis=0)
    return out.astype(np.float32)
